# version 04 (final)
# baseline (speedup 1.0000x reference)
from contextlib import ExitStack

import numpy as np
import ml_dtypes
import concourse.bass as bass
import concourse.mybir as mybir
from concourse.bass_utils import run_bass_kernel_spmd

F32 = mybir.dt.float32
BF16 = mybir.dt.bfloat16
I32 = mybir.dt.int32
AF = mybir.ActivationFunctionType
ALU = mybir.AluOpType
AX = mybir.AxisListType

T = 4096
D = 1024
NT = T // 128
ND = D // 128
LN_EPS = 1e-5
F32EPS = float(np.finfo(np.float32).eps)

bf16 = ml_dtypes.bfloat16



class Buf:
    __slots__ = ("w", "r", "name")

    def __init__(self, name=""):
        self.w = None
        self.r = []
        self.name = name


class Ctx:
    ENG2SEM = dict(tensor="pe", vector="dve", scalar="act", gpsimd="pool", sync="sp")
    N_DMA_SEMS = 40

    def __init__(self, nc):
        self.nc = nc
        self.sem = {}
        for name in ("pe", "dve", "act", "pool", "sp"):
            self.sem[name] = [nc.semaphore("sem_" + name).__enter__(), 0]
        for i in range(self.N_DMA_SEMS):
            self.sem["q%d" % i] = [nc.semaphore("semq%d" % i).__enter__(), 0]
        self._dma_rr = 0
        self.wm = {}
        self.stats = dict(instr=0, dma=0)
        self.inputs = {}
        self.input_kinds = {}
        self.consts = {}
        self._caps = {}
        self._cap_pool = nc.alloc_sbuf_tensor("cap_pool", [128, 64], F32)
        self._cap_n = 0

    def win(self, name, shape, dtype, kind="ExternalInput"):
        if name in self.inputs:
            return self.inputs[name]
        h = self.nc.dram_tensor(name, list(shape), dtype, kind=kind)
        b = Buf(name)
        self.inputs[name] = (h, b)
        self.input_kinds[name] = kind
        return h, b

    def external_input_names(self):
        return [n for n, k in self.input_kinds.items() if k == "ExternalInput"]

    def new_dma_sem(self):
        s = "q%d" % (self._dma_rr % self.N_DMA_SEMS)
        self._dma_rr += 1
        return s

    def cap(self, v):
        v = float(v)
        if v in self._caps:
            return self._caps[v]
        i = self._cap_n
        self._cap_n += 1
        assert i < 64
        b = Buf("cap%g" % v)
        ap = self._cap_pool[:, i:i + 1]
        self.op("vector", lambda e: e.memset(ap, v), writes=(b,))
        self._caps[v] = (ap, b)
        return ap, b

    def eng(self, e):
        return getattr(self.nc, e)

    def wait(self, engname, dep):
        if dep is None:
            return
        s, v = dep
        if v <= 0:
            return
        k = (engname, s)
        if self.wm.get(k, 0) >= v:
            return
        self.eng(engname).wait_ge(self.sem[s][0], v)
        self.wm[k] = v

    def waits(self, engname, reads=(), writes=()):
        for b in reads:
            self.wait(engname, b.w)
        for b in writes:
            self.wait(engname, b.w)
            for r in b.r:
                self.wait(engname, r)

    def _update(self, val, reads, writes):
        for b in reads:
            b.r.append(val)
        for b in writes:
            b.w = val
            b.r = []

    def done(self, engname, instr, reads=(), writes=()):
        semname = self.ENG2SEM[engname]
        rec = self.sem[semname]
        rec[1] += 1
        instr.then_inc(rec[0], 1)
        val = (semname, rec[1])
        self._update(val, reads, writes)
        return val

    def op(self, engname, fn, reads=(), writes=()):
        self.waits(engname, reads, writes)
        inst = fn(self.eng(engname))
        self.stats["instr"] += 1
        return self.done(engname, inst, reads, writes)

    def dma(self, engname, out_ap, in_ap, reads=(), writes=(), sem=None, **kw):
        assert sem is not None
        self.waits(engname, reads, writes)
        inst = self.eng(engname).dma_start(out=out_ap, in_=in_ap, **kw)
        self.stats["dma"] += 1
        rec = self.sem[sem]
        rec[1] += 16
        inst.then_inc(rec[0], 16)
        val = (sem, rec[1])
        self._update(val, reads, writes)
        return val

    def mm(self, ps_buf, ps_ap, pairs, reads=(), accum=False):
        self.waits("tensor", reads, (ps_buf,))
        n = len(pairs)
        inst = None
        for i, (l, r) in enumerate(pairs):
            inst = self.nc.tensor.matmul(
                ps_ap, l, r, start=(i == 0 and not accum), stop=(i == n - 1))
            self.stats["instr"] += 1
        return self.done("tensor", inst, reads, (ps_buf,))

    def transpose(self, ps_buf, ps_ap, in_ap, reads=(), nparts=128, f32=False):
        self.waits("tensor", reads, (ps_buf,))
        ident = self.consts["identf32" if f32 else "identbf"]
        inst = self.nc.tensor.transpose(ps_ap, in_ap,
                                        ident[0:nparts, 0:nparts])
        self.stats["instr"] += 1
        return self.done("tensor", inst, reads, (ps_buf,))

    def barrier(self):
        for e in ("tensor", "vector", "scalar", "gpsimd", "sync"):
            for s, rec in self.sem.items():
                self.wait(e, (s, rec[1]))


def make_consts(c):
    nc = c.nc
    identh, identhB = c.win("c_ident", [128, 128], F32)
    ident = nc.alloc_sbuf_tensor("identbf", [128, 128], BF16)
    ones = nc.alloc_sbuf_tensor("onesrow", [1, 128], BF16)
    onescol = nc.alloc_sbuf_tensor("onescol", [128, 1], BF16)
    onesf = nc.alloc_sbuf_tensor("onesrowf", [1, 512], F32)
    q = c.new_dma_sem()
    identB = Buf("ident")
    c.dma("gpsimd", ident[:], identh.ap()[:, :], reads=(identhB,), writes=(identB,), sem=q)
    c.op("vector", lambda e: e.memset(ones[:], 1.0), writes=(Buf(),))
    c.op("vector", lambda e: e.memset(onescol[:], 1.0), writes=(Buf(),))
    c.op("vector", lambda e: e.memset(onesf[:], 1.0), writes=(Buf(),))
    identf32 = nc.alloc_sbuf_tensor("identf32", [128, 128], F32)
    identf32B = Buf("identf32")
    c.dma("sync", identf32[:], identh.ap()[:, :], reads=(identhB,),
          writes=(identf32B,), sem=q)
    c.consts["identbf"] = ident
    c.consts["identbf_buf"] = identB
    c.consts["identf32"] = identf32
    c.consts["identf32_buf"] = identf32B
    c.consts["ones"] = ones
    c.consts["onescol"] = onescol
    c.consts["onesf"] = onesf
    c.barrier()


HOSTPREP = {}


def hostprep(name):
    def reg(fn):
        HOSTPREP[name] = fn
        return fn
    return reg


@hostprep("c_ident")
def _hp_ident(inp):
    return np.eye(128, dtype=np.float32)



def col_to_row(c, sb_pool, col_ap, ncols, buf, ps, psB, row, rowB):
    c.transpose(psB, ps[0:1, 0:ncols], col_ap, reads=(buf,))
    c.op("vector", lambda e: e.tensor_copy(row[0:1, 0:ncols], ps[0:1, 0:ncols]),
         reads=(psB,), writes=(rowB,))


def bcast_row(c, row_ap, n, rowB, ps_buf, ps_ap, nparts=128):
    ones = c.consts["ones"]
    c.mm(ps_buf, ps_ap, [(ones[0:1, 0:nparts], row_ap)],
         reads=(rowB, c.consts["identbf_buf"]))



def st_norm(c, pfx, x_h, x_B, xnT_h, xnT_B,
            xn_tm=None, xnrT=None):
    nc = c.nc
    with ExitStack() as ex:
        sb = lambda n, s, d: ex.enter_context(nc.sbuf_tensor(pfx + n, s, d))
        pb = lambda n, s, d: ex.enter_context(nc.psum_tensor(pfx + n, s, d))
        xb = [sb("x%d" % i, [128, D], F32) for i in range(2)]
        xbB = [Buf() for _ in range(2)]
        sq = [sb("sq%d" % i, [128, D], F32) for i in range(2)]
        sqB_ = [Buf() for _ in range(2)]
        st = [sb("st%d" % i, [128, 8], F32) for i in range(2)]
        stB_ = [Buf() for _ in range(2)]
        xn = [sb("xn%d" % i, [128, D], BF16) for i in range(2)]
        xnB_ = [Buf() for _ in range(2)]
        xnT = sb("xnT", [128, ND * T], BF16)
        xnTB = [Buf() for _ in range(ND)]
        ps = [pb("ps%d" % i, [128, 128], BF16) for i in range(2)]
        psB = [Buf() for _ in range(2)]
        do_r = xnrT is not None
        do_f = xn_tm is not None
        if do_r:
            xnr = sb("xnr", [128, D], BF16)
            xnrB = Buf()
            xnrTs = sb("xnrT", [128, ND * T], BF16)
            xnrTsB = [Buf() for _ in range(ND)]
            ps2 = [pb("ps2%d" % i, [128, 128], BF16) for i in range(2)]
            ps2B = [Buf() for _ in range(2)]
        if do_f:
            xnf = sb("xnf", [128, D], F32)
            xnfB = Buf()
        qs = c.new_dma_sem()
        qo = c.new_dma_sem()

        for i in range(NT):
            j = i % 2
            b = xbB[j]
            xt = xb[j]
            sq_, sqB = sq[j], sqB_[j]
            st_, stB = st[j], stB_[j]
            xn_, xnB = xn[j], xnB_[j]
            c.dma("sync", xt[:], x_h.ap()[i * 128:(i + 1) * 128, :],
                  reads=(x_B[i],), writes=(b,), sem=qs)
            c.op("vector", lambda e: e.reduce_sum(st_[:, 0:1], xt[:], axis=AX.X),
                 reads=(b,), writes=(stB,))
            c.op("scalar", lambda e: e.activation(out=sq_[:], in_=xt[:], func=AF.Square),
                 reads=(b,), writes=(sqB,))
            c.op("vector", lambda e: e.reduce_sum(st_[:, 1:2], sq_[:], axis=AX.X),
                 reads=(sqB,), writes=(stB,))
            c.op("vector", lambda e: e.tensor_scalar(st_[:, 2:3], st_[:, 0:1],
                                                     -1.0 / D, None, ALU.mult),
                 reads=(stB,), writes=(stB,))
            c.op("vector", lambda e: e.tensor_tensor(st_[:, 3:4], st_[:, 2:3],
                                                     st_[:, 2:3], ALU.mult),
                 reads=(stB,), writes=(stB,))
            c.op("vector", lambda e: e.scalar_tensor_tensor(st_[:, 4:5], st_[:, 1:2],
                                                            1.0 / D, st_[:, 3:4],
                                                            ALU.mult, ALU.subtract),
                 reads=(stB,), writes=(stB,))
            epap, epB = c.cap(LN_EPS)
            c.op("scalar", lambda e: e.activation(out=st_[:, 5:6], in_=st_[:, 4:5],
                                                  func=AF.Sqrt, bias=epap),
                 reads=(stB, epB), writes=(stB,))
            c.op("vector", lambda e: e.reciprocal(st_[:, 6:7], st_[:, 5:6]),
                 reads=(stB,), writes=(stB,))
            c.op("vector", lambda e: e.tensor_scalar(xn_[:], xt[:], st_[:, 2:3],
                                                     st_[:, 6:7], ALU.add, ALU.mult),
                 reads=(b, stB), writes=(xnB,))
            if do_f:
                c.op("scalar", lambda e: e.activation(out=xnf[:], in_=xt[:],
                                                      func=AF.Identity,
                                                      bias=st_[:, 2:3]),
                     reads=(b, stB), writes=(xnfB,))
                c.op("scalar", lambda e: e.activation(out=xnf[:], in_=xnf[:],
                                                      func=AF.Identity,
                                                      scale=st_[:, 6:7]),
                     reads=(stB,), writes=(xnfB,))
                c.dma("sync", xn_tm[0].ap()[i * 128:(i + 1) * 128, :], xnf[:],
                      reads=(xnfB,), writes=(xn_tm[1],), sem=qo)
            if do_r:
                c.op("scalar", lambda e: e.activation(out=sq_[:], in_=xn_[:],
                                                      func=AF.Square),
                     reads=(xnB,), writes=(sqB,))
                c.op("vector", lambda e: e.reduce_sum(st_[:, 1:2], sq_[:], axis=AX.X),
                     reads=(sqB,), writes=(stB,))
                e2ap, e2B = c.cap(F32EPS)
                c.op("scalar", lambda e: e.activation(out=st_[:, 5:6], in_=st_[:, 1:2],
                                                      func=AF.Sqrt, scale=1.0 / D,
                                                      bias=e2ap),
                     reads=(stB, e2B), writes=(stB,))
                c.op("vector", lambda e: e.reciprocal(st_[:, 7:8], st_[:, 5:6]),
                     reads=(stB,), writes=(stB,))
                c.op("vector", lambda e: e.tensor_scalar(xnr[:], xn_[:], st_[:, 7:8],
                                                         None, ALU.mult),
                     reads=(xnB, stB), writes=(xnrB,))
            for d in range(ND):
                p = d % 2
                c.transpose(psB[p], ps[p][:, :], xn_[:, d * 128:(d + 1) * 128],
                            reads=(xnB,))
                eng_d = "vector" if d % 2 == 0 else "scalar"
                if eng_d == "vector":
                    c.op("vector", lambda e, d=d, p=p: e.tensor_copy(
                        xnT[:, d * T + i * 128: d * T + (i + 1) * 128], ps[p][:, :]),
                        reads=(psB[p],), writes=(xnTB[d],))
                else:
                    c.op("scalar", lambda e, d=d, p=p: e.copy(
                        xnT[:, d * T + i * 128: d * T + (i + 1) * 128], ps[p][:, :]),
                        reads=(psB[p],), writes=(xnTB[d],))
                if do_r:
                    c.transpose(ps2B[p], ps2[p][:, :], xnr[:, d * 128:(d + 1) * 128],
                                reads=(xnrB,))
                    c.op("scalar", lambda e, d=d, p=p: e.copy(
                        xnrTs[:, d * T + i * 128: d * T + (i + 1) * 128], ps2[p][:, :]),
                        reads=(ps2B[p],), writes=(xnrTsB[d],))
        for d in range(ND):
            c.dma("sync", xnT_h.ap()[d * 128:(d + 1) * 128, :],
                  xnT[:, d * T:(d + 1) * T],
                  reads=(xnTB[d],), writes=(xnT_B,), sem=qo)
            if do_r:
                c.dma("scalar", xnrT[0].ap()[d * 128:(d + 1) * 128, :],
                      xnrTs[:, d * T:(d + 1) * T],
                      reads=(xnrTsB[d],), writes=(xnrT[1],), sem=qo)
        c.barrier()



def st_l1_dequant(c, pfx, io):
    nc = c.nc
    wgT_h, wgT_B = c.win("l1_wgT_i", [D, 3072], I32)
    wuT_h, wuT_B = c.win("l1_wuT_i", [D, 3072], I32)
    wdT_h, wdT_B = c.win("l1_wdT_i", [3072, D], I32)
    jobs = [(wgT_h, wgT_B, io["l1_wgT"], 8, 3072),
            (wuT_h, wuT_B, io["l1_wuT"], 8, 3072),
            (wdT_h, wdT_B, io["l1_wdT"], 24, D)]
    with ExitStack() as ex:
        sb = lambda n, s, d: ex.enter_context(nc.sbuf_tensor(pfx + n, s, d))
        tf = [sb("f%d" % i, [128, 3072], F32) for i in range(2)]
        tb = [sb("b%d" % i, [128, 3072], BF16) for i in range(2)]
        tfB = [Buf() for _ in range(2)]
        tbB = [Buf() for _ in range(2)]
        q = c.new_dma_sem()
        qo = c.new_dma_sem()
        for src_h, src_B, (dst_h, dst_B), nsl, w in jobs:
            for k in range(nsl):
                j = (k % 2)
                c.dma("gpsimd", tf[j][:, 0:w], src_h.ap()[k * 128:(k + 1) * 128, :],
                      reads=(src_B,), writes=(tfB[j],), sem=q)
                c.op("vector", lambda e, j=j, w=w: e.tensor_scalar(
                    tb[j][:, 0:w], tf[j][:, 0:w], 1.0 / 127.0, None, ALU.mult),
                    reads=(tfB[j],), writes=(tbB[j],))
                c.dma("sync", dst_h.ap()[k * 128:(k + 1) * 128, :], tb[j][:, 0:w],
                      reads=(tbB[j],), writes=(dst_B,), sem=qo)
        c.barrier()


@hostprep("l1_wgT_i")
def _hp_l1wg(inp):
    return np.ascontiguousarray(inp["L1_ffn_w_gate"].T)


@hostprep("l1_wuT_i")
def _hp_l1wu(inp):
    return np.ascontiguousarray(inp["L1_ffn_w_up"].T)


@hostprep("l1_wdT_i")
def _hp_l1wd(inp):
    return np.ascontiguousarray(inp["L1_ffn_w_down"].T)



def st_l1_ffn(c, pfx, io, x_h, x_B, xnT_h, xnT_B, alpha=0.5):
    nc = c.nc
    H = 3072
    NH = H // 128
    wgT_h, wgT_B = io["l1_wgT"]
    wuT_h, wuT_B = io["l1_wuT"]
    wdT_h, wdT_B = io["l1_wdT"]
    TC = 512
    NC = T // TC
    with ExitStack() as ex:
        sb = lambda n, s, d: ex.enter_context(nc.sbuf_tensor(pfx + n, s, d))
        pb = lambda n, s, d: ex.enter_context(nc.psum_tensor(pfx + n, s, d))
        wg = sb("wg", [128, 8 * H], BF16)
        wu = sb("wu", [128, 8 * H], BF16)
        wgB = Buf("wg")
        wuB = Buf("wu")
        wd = [sb("wd%d" % i, [128, D], BF16) for i in range(4)]
        wdB = [Buf() for _ in range(4)]
        xc = [sb("xc%d" % i, [128, 8 * TC], BF16) for i in range(2)]
        xcB = [Buf() for _ in range(2)]
        hb = sb("hb", [128, NH * TC], BF16)
        hbB = [Buf() for _ in range(NH)]
        gs = sb("gs", [128, TC], BF16)
        gsB = Buf()
        xr = sb("xr", [128, 4 * D], F32)
        xrB = [Buf() for _ in range(4)]
        ev = [sb("ev%d" % i, [128, 512], F32) for i in range(2)]
        evB = [Buf() for _ in range(2)]
        pso = [pb("pso%d" % i, [128, 512], F32) for i in range(8)]
        psoB = [Buf() for _ in range(8)]
        psg, psu = pso[6], pso[7]
        psgB, psuB = psoB[6], psoB[7]
        qw = c.new_dma_sem()
        qwd = c.new_dma_sem()
        qx = c.new_dma_sem()
        qr = c.new_dma_sem()
        qo = c.new_dma_sem()

        for k in range(8):
            c.dma("gpsimd", wg[:, k * H:(k + 1) * H],
                  wgT_h.ap()[k * 128:(k + 1) * 128, :],
                  reads=(wgT_B,), writes=(wgB,), sem=qw)
            c.dma("gpsimd", wu[:, k * H:(k + 1) * H],
                  wuT_h.ap()[k * 128:(k + 1) * 128, :],
                  reads=(wuT_B,), writes=(wuB,), sem=qw)

        for ci in range(NC):
            t0 = ci * TC
            jc = ci % 2
            for k in range(ND):
                c.dma("sync", xc[jc][:, k * TC:(k + 1) * TC],
                      xnT_h.ap()[k * 128:(k + 1) * 128, t0:t0 + TC],
                      reads=(xnT_B,), writes=(xcB[jc],), sem=qx)
            for tt in range(4):
                ig = ci * 4 + tt
                c.dma("sync", xr[:, tt * D:(tt + 1) * D],
                      x_h.ap()[ig * 128:(ig + 1) * 128, :],
                      reads=(x_B[ig],), writes=(xrB[tt],), sem=qr)
            for m in range(NH):
                c.mm(psgB, psg[:, 0:TC],
                     [(wg[:, k * H + m * 128:k * H + (m + 1) * 128],
                       xc[jc][:, k * TC:(k + 1) * TC]) for k in range(8)],
                     reads=(wgB, xcB[jc]))
                c.mm(psuB, psu[:, 0:TC],
                     [(wu[:, k * H + m * 128:k * H + (m + 1) * 128],
                       xc[jc][:, k * TC:(k + 1) * TC]) for k in range(8)],
                     reads=(wuB, xcB[jc]))
                c.op("scalar", lambda e: e.activation(out=gs[:], in_=psg[:, 0:TC],
                                                      func=AF.Silu),
                     reads=(psgB,), writes=(gsB,))
                c.op("vector", lambda e, m=m: e.tensor_tensor(
                    hb[:, m * TC:(m + 1) * TC], gs[:], psu[:, 0:TC], ALU.mult),
                    reads=(gsB, psuB), writes=(hbB[m],))
            for k in range(NH):
                kw = k % 4
                c.dma("gpsimd", wd[kw][:], wdT_h.ap()[k * 128:(k + 1) * 128, :],
                      reads=(wdT_B,), writes=(wdB[kw],), sem=qwd)
                c.waits("tensor", reads=(wdB[kw], hbB[k]),
                        writes=psoB if k == 0 else ())
                inst = None
                for tt in range(4):
                    for half in range(2):
                        inst = nc.tensor.matmul(
                            pso[tt * 2 + half][:, :],
                            hb[:, k * TC + tt * 128:k * TC + tt * 128 + 128],
                            wd[kw][:, half * 512:(half + 1) * 512],
                            start=(k == 0), stop=(k == NH - 1))
                        c.stats["instr"] += 1
                val = c.done("tensor", inst, reads=(wdB[kw], hbB[k]), writes=())
                if k == NH - 1:
                    for b in psoB:
                        b.w = val
                        b.r = []
            for tt in range(4):
                ig = ci * 4 + tt
                for half in range(2):
                    je = half
                    c.op("vector", lambda e, tt=tt, half=half, je=je: e.scalar_tensor_tensor(
                        ev[je][:], pso[tt * 2 + half][:, :], float(alpha),
                        xr[:, tt * D + half * 512:tt * D + (half + 1) * 512],
                        ALU.mult, ALU.add),
                        reads=(psoB[tt * 2 + half], xrB[tt]), writes=(evB[je],))
                    c.dma("sync", x_h.ap()[ig * 128:(ig + 1) * 128,
                                           half * 512:(half + 1) * 512],
                          ev[je][:],
                          reads=(evB[je],), writes=(x_B[ig],), sem=qo)
        c.barrier()



def st_router(c, pfx, cfg, xn_tm_h, xn_tm_B, w_all, w_allB, wT_all, wT_allB):
    nc = c.nc
    with ExitStack() as ex:
        sb = lambda n, s, d: ex.enter_context(nc.sbuf_tensor(pfx + n, s, d))
        pb = lambda n, s, d: ex.enter_context(nc.psum_tensor(pfx + n, s, d))
        wr_h, wr_B = c.win(cfg["wr"], [D, 8], F32)
        wr = sb("wr", [128, 8 * ND], F32)
        wrB = Buf()
        q = c.new_dma_sem()
        for k in range(ND):
            c.dma("gpsimd", wr[:, k * 8:(k + 1) * 8],
                  wr_h.ap()[k * 128:(k + 1) * 128, :],
                  reads=(wr_B,), writes=(wrB,), sem=q)
        brb = None
        if cfg.get("br"):
            br_h, br_B = c.win(cfg["br"], [1, 8], F32)
            brrow = sb("brrow", [1, 8], F32)
            brrowB = Buf()
            c.dma("gpsimd", brrow[:, :], br_h.ap()[:, :], reads=(br_B,),
                  writes=(brrowB,), sem=q)
            psb = pb("psb", [128, 8], F32)
            psbB = Buf()
            c.mm(psbB, psb[:, :], [(c.consts["onesf"][0:1, 0:128],
                                    brrow[0:1, 0:8])], reads=(brrowB,))
            brb = sb("brb", [128, 8], F32)
            brbB = Buf()
            c.op("vector", lambda e: e.tensor_copy(brb[:], psb[:]),
                 reads=(psbB,), writes=(brbB,))
        xm = sb("xm", [128, D], F32)
        xmB = Buf()
        xc = sb("xc", [128, ND * 128], F32)
        xcB = Buf()
        pstf = pb("pstf", [128, 128], F32)
        pstfB = Buf()
        identf32 = c.consts["identf32"]
        lg = sb("lg", [128, 8], F32)
        lgB = Buf()
        stt = sb("stt", [128, 8], F32)
        sttB = Buf()
        tmp = sb("tmp", [128, 8], F32)
        tmpB = Buf()
        wbf = sb("wbf", [128, 8], BF16)
        wbfB = Buf()
        psl = pb("psl", [128, 8], F32)
        pslB = Buf()
        pst = pb("pst", [8, 128], BF16)
        pstB = Buf()
        qx = c.new_dma_sem()
        for i in range(NT):
            c.dma("sync", xm[:], xn_tm_h.ap()[i * 128:(i + 1) * 128, :],
                  reads=(xn_tm_B,), writes=(xmB,), sem=qx)
            for k in range(ND):
                c.waits("tensor", (xmB, c.consts["identf32_buf"]), (pstfB,))
                inst = nc.tensor.transpose(pstf[:, :], xm[:, k * 128:(k + 1) * 128],
                                           identf32[:, :])
                c.stats["instr"] += 1
                c.done("tensor", inst, (xmB,), (pstfB,))
                c.op("vector", lambda e, k=k: e.tensor_copy(
                    xc[:, k * 128:(k + 1) * 128], pstf[:, :]),
                    reads=(pstfB,), writes=(xcB,))
            c.mm(pslB, psl[:, :],
                 [(xc[:, k * 128:(k + 1) * 128], wr[:, k * 8:(k + 1) * 8])
                  for k in range(ND)],
                 reads=(xcB, wrB))
            if brb is not None:
                c.op("vector", lambda e: e.tensor_add(lg[:], psl[:], brb[:]),
                     reads=(pslB, brbB), writes=(lgB,))
            else:
                c.op("vector", lambda e: e.tensor_copy(lg[:], psl[:]),
                     reads=(pslB,), writes=(lgB,))
            if cfg["kind"] == "sig2":
                c.op("scalar", lambda e: e.activation(out=lg[:], in_=lg[:],
                                                      func=AF.Sigmoid),
                     reads=(lgB,), writes=(lgB,))
            c.op("vector", lambda e: e.reduce_max(stt[:, 0:1], lg[:], axis=AX.X,
                                                  negate=True),
                 reads=(lgB,), writes=(sttB,))
            c.op("vector", lambda e: e.tensor_scalar(tmp[:], lg[:], stt[:, 0:1],
                                                     None, ALU.add),
                 reads=(lgB, sttB), writes=(tmpB,))
            c.op("vector", lambda e: e.tensor_scalar(tmp[:], tmp[:], 0.0, None,
                                                     ALU.is_ge),
                 reads=(tmpB,), writes=(tmpB,))
            c.op("vector", lambda e: e.reduce_sum(stt[:, 2:3], tmp[:], axis=AX.X),
                 reads=(tmpB,), writes=(sttB,))
            c.op("vector", lambda e: e.scalar_tensor_tensor(tmp[:], tmp[:], -1e38,
                                                            lg[:], ALU.mult, ALU.add),
                 reads=(tmpB, lgB), writes=(tmpB,))
            c.op("vector", lambda e: e.reduce_max(stt[:, 1:2], tmp[:], axis=AX.X),
                 reads=(tmpB,), writes=(sttB,))
            c.op("vector", lambda e: e.tensor_scalar(stt[:, 2:3], stt[:, 2:3], 2.0,
                                                     None, ALU.is_ge),
                 reads=(sttB,), writes=(sttB,))
            c.op("vector", lambda e: e.scalar_tensor_tensor(stt[:, 3:4], stt[:, 0:1],
                                                            -1.0, stt[:, 1:2],
                                                            ALU.mult, ALU.subtract),
                 reads=(sttB,), writes=(sttB,))
            c.op("vector", lambda e: e.tensor_tensor(stt[:, 3:4], stt[:, 3:4],
                                                     stt[:, 2:3], ALU.mult),
                 reads=(sttB,), writes=(sttB,))
            c.op("vector", lambda e: e.tensor_tensor(stt[:, 3:4], stt[:, 3:4],
                                                     stt[:, 1:2], ALU.add),
                 reads=(sttB,), writes=(sttB,))
            if cfg["kind"] == "softmax2":
                c.op("scalar", lambda e: e.activation(out=tmp[:], in_=lg[:],
                                                      func=AF.Exp, bias=stt[:, 0:1]),
                     reads=(lgB, sttB), writes=(tmpB,))
                src, srcB = tmp, tmpB
            else:
                src, srcB = lg, lgB
            c.op("vector", lambda e: e.tensor_scalar(wbf[:], lg[:], stt[:, 3:4],
                                                     None, ALU.is_ge),
                 reads=(lgB, sttB), writes=(wbfB,))
            c.op("vector", lambda e, src=src: e.tensor_tensor(tmp[:], src[:], wbf[:],
                                                              ALU.mult),
                 reads=(srcB, wbfB), writes=(tmpB,))
            if cfg["kind"] == "softmax2":
                c.op("vector", lambda e: e.reduce_sum(stt[:, 4:5], tmp[:], axis=AX.X),
                     reads=(tmpB,), writes=(sttB,))
            else:
                c.op("vector", lambda e: e.reduce_sum(stt[:, 4:5], lg[:], axis=AX.X),
                     reads=(lgB,), writes=(sttB,))
            c.op("vector", lambda e: e.reciprocal(stt[:, 5:6], stt[:, 4:5]),
                 reads=(sttB,), writes=(sttB,))
            c.op("vector", lambda e, i=i: e.tensor_scalar(
                w_all[:, i * 8:(i + 1) * 8], tmp[:], stt[:, 5:6], None, ALU.mult),
                reads=(tmpB, sttB), writes=(w_allB[i],))
            c.op("vector", lambda e, i=i: e.tensor_copy(wbf[:],
                                                        w_all[:, i * 8:(i + 1) * 8]),
                 reads=(w_allB[i],), writes=(wbfB,))
            c.transpose(pstB, pst[:, :], wbf[:, :], reads=(wbfB,))
            c.op("vector", lambda e, i=i: e.tensor_copy(
                wT_all[0:8, i * 128:(i + 1) * 128], pst[0:8, :]),
                reads=(pstB,), writes=(wT_allB,))
        c.barrier()


def st_expert_ffn(c, pfx, cfg, xnT_h, xnT_B, acc_h, accB, w_all, w_allB,
                  wT_all, wT_allB):
    nc = c.nc
    HEXP = 4096
    NE = cfg["n_routed"] + cfg["n_shared"]
    W1_h, W1_B = c.win(cfg["W1"], [NE, D, HEXP], F32)
    W2_h, W2_B = c.win(cfg["W2"], [NE, HEXP, D], F32)
    b1_h, b1_B = c.win(cfg["b1cols"], [NE, 128, 32], F32)
    b2_h, b2_B = c.win(cfg["b2"], [NE, D], F32)
    gc_r = gc_s = None
    if cfg.get("gcols_r"):
        gr_h, gr_B = c.win(cfg["gcols_r"], [128, 8], F32)
        gs_h, gs_B = c.win(cfg["gcols_s"], [128, 8], F32)
    TC = 512
    NCH = T // TC
    with ExitStack() as ex:
        sb = lambda n, s, d: ex.enter_context(nc.sbuf_tensor(pfx + n, s, d))
        pb = lambda n, s, d: ex.enter_context(nc.psum_tensor(pfx + n, s, d))
        w1 = [sb("w1%d" % i, [128, 8 * 2048], BF16) for i in range(2)]
        w1B = [Buf() for _ in range(2)]
        w2 = [sb("w2%d" % i, [128, 16 * D], BF16) for i in range(2)]
        w2B = [Buf() for _ in range(2)]
        gcol = sb("gcol", [128, 8], F32)
        gcolB = Buf()
        b1c = sb("b1c", [128, 32], F32)
        b1cB = Buf()
        b2r = sb("b2r", [8, D], BF16)
        b2rB = Buf()
        b2s = [sb("b2s%d" % s, [1, D], BF16) for s in range(cfg["n_shared"])]
        b2sB = [Buf() for _ in range(cfg["n_shared"])]
        xc = [sb("xc%d" % i, [128, ND * TC], BF16) for i in range(2)]
        xcB = [Buf() for _ in range(2)]
        hb = sb("hb", [128, 16 * TC], BF16)
        hbB = [Buf() for _ in range(16)]
        ev = [sb("ev%d" % i, [128, 512], F32) for i in range(4)]
        evB = [Buf() for _ in range(4)]
        pso = [pb("pso%d" % i, [128, 512], F32) for i in range(8)]
        psoB = [Buf() for _ in range(8)]
        psh2 = [pso[6], pso[7]]
        psh2B = [psoB[6], psoB[7]]
        qw = c.new_dma_sem()
        qx = c.new_dma_sem()
        qa = c.new_dma_sem()
        qg = c.new_dma_sem()

        c.dma("gpsimd", b2r[:, :], b2_h.ap()[0:cfg["n_routed"], :],
              reads=(b2_B,), writes=(b2rB,), sem=qw)
        for s in range(cfg["n_shared"]):
            c.dma("gpsimd", b2s[s][:, :],
                  b2_h.ap()[cfg["n_routed"] + s:cfg["n_routed"] + s + 1, :],
                  reads=(b2_B,), writes=(b2sB[s],), sem=qw)
        ones = c.consts["ones"]
        onesB = Buf()
        for i in range(NT):
            for half in range(2):
                pairs = [(wT_all[0:cfg["n_routed"], i * 128:(i + 1) * 128],
                          b2r[0:cfg["n_routed"], half * 512:(half + 1) * 512])]
                rds = [wT_allB, b2rB]
                for s in range(cfg["n_shared"]):
                    pairs.append((ones[0:1, 0:128],
                                  b2s[s][0:1, half * 512:(half + 1) * 512]))
                    rds.append(b2sB[s])
                bank = (i * 2 + half) % 6
                c.mm(psoB[bank], pso[bank][:, :], pairs, reads=tuple(rds))
                je = (i * 2 + half) % 4
                c.op("vector", lambda e, bank=bank, je=je: e.tensor_copy(
                    ev[je][:], pso[bank][:, :]), reads=(psoB[bank],), writes=(evB[je],))
                c.dma("sync", acc_h.ap()[i * 128:(i + 1) * 128,
                                         half * 512:(half + 1) * 512],
                      ev[je][:], reads=(evB[je],), writes=(accB[i * 2 + half],),
                      sem=qa)

        for e_idx in range(NE):
            routed = e_idx < cfg["n_routed"]
            for half in range(2):
                h0 = half * 2048
                jw = (e_idx * 2 + half) % 2
                w1c, w1cB = w1[jw], w1B[jw]
                w2c, w2cB = w2[jw], w2B[jw]
                for k in range(8):
                    c.dma("gpsimd", w1c[:, k * 2048:(k + 1) * 2048],
                          W1_h.ap()[e_idx, k * 128:(k + 1) * 128, h0:h0 + 2048],
                          reads=(W1_B,), writes=(w1cB,), sem=qw)
                if cfg.get("gcols_r"):
                    g_h = gr_h if routed else gs_h
                    g_B = gr_B if routed else gs_B
                    c.dma("sync", gcol[:, :], g_h.ap()[:, :], reads=(g_B,),
                          writes=(gcolB,), sem=qg)
                    for k in range(8):
                        c.op("vector", lambda e, k=k, w1c=w1c: e.tensor_scalar(
                            w1c[:, k * 2048:(k + 1) * 2048],
                            w1c[:, k * 2048:(k + 1) * 2048],
                            gcol[:, k:k + 1], None, ALU.mult),
                            reads=(w1cB, gcolB), writes=(w1cB,))
                for k in range(16):
                    c.dma("gpsimd", w2c[:, k * D:(k + 1) * D],
                          W2_h.ap()[e_idx, h0 + k * 128:h0 + (k + 1) * 128, :],
                          reads=(W2_B,), writes=(w2cB,), sem=qw)
                c.dma("sync", b1c[:, :], b1_h.ap()[e_idx, :, :], reads=(b1_B,),
                      writes=(b1cB,), sem=qg)
                for ci in range(NCH):
                    t0 = ci * TC
                    jc = ci % 2
                    for k in range(ND):
                        c.dma("sync", xc[jc][:, k * TC:(k + 1) * TC],
                              xnT_h.ap()[k * 128:(k + 1) * 128, t0:t0 + TC],
                              reads=(xnT_B,), writes=(xcB[jc],), sem=qx)
                    for m in range(16):
                        pj = m % 2
                        c.mm(psh2B[pj], psh2[pj][:, :],
                             [(w1c[:, k * 2048 + m * 128:k * 2048 + (m + 1) * 128],
                               xc[jc][:, k * TC:(k + 1) * TC]) for k in range(8)],
                             reads=(w1cB, xcB[jc]))
                        c.op("scalar", lambda e, m=m, half=half, pj=pj: e.activation(
                            out=hb[:, m * TC:(m + 1) * TC], in_=psh2[pj][:, :],
                            func=AF.Gelu, bias=b1c[:, half * 16 + m:half * 16 + m + 1]),
                            reads=(psh2B[pj], b1cB), writes=(hbB[m],))
                    for k in range(16):
                        c.waits("tensor", reads=(w2cB, hbB[k]),
                                writes=psoB if k == 0 else ())
                        inst = None
                        for tt in range(4):
                            for dh in range(2):
                                inst = nc.tensor.matmul(
                                    pso[tt * 2 + dh][:, :],
                                    hb[:, k * TC + tt * 128:k * TC + tt * 128 + 128],
                                    w2c[:, k * D + dh * 512:k * D + (dh + 1) * 512],
                                    start=(k == 0), stop=(k == 15))
                                c.stats["instr"] += 1
                        val = c.done("tensor", inst, reads=(w2cB, hbB[k]), writes=())
                        if k == 15:
                            for b in psoB:
                                b.w = val
                                b.r = []
                    for tt in range(4):
                        ig = ci * 4 + tt
                        for dh in range(2):
                            je = (tt * 2 + dh) % 4
                            if routed:
                                c.op("vector", lambda e, tt=tt, dh=dh, je=je, ig=ig:
                                     e.tensor_scalar(
                                         ev[je][:], pso[tt * 2 + dh][:, :],
                                         w_all[:, ig * 8 + e_idx:ig * 8 + e_idx + 1],
                                         None, ALU.mult),
                                     reads=(psoB[tt * 2 + dh], w_allB[ig]),
                                     writes=(evB[je],))
                            else:
                                c.op("vector", lambda e, tt=tt, dh=dh, je=je:
                                     e.tensor_copy(ev[je][:], pso[tt * 2 + dh][:, :]),
                                     reads=(psoB[tt * 2 + dh],), writes=(evB[je],))
                            c.dma("gpsimd",
                                  acc_h.ap()[ig * 128:(ig + 1) * 128,
                                             dh * 512:(dh + 1) * 512],
                                  ev[je][:], reads=(evB[je],),
                                  writes=(accB[ig * 2 + dh],), sem=qa,
                                  accum_op=ALU.add)
        c.barrier()



@hostprep("l2_Wr")
def _hp_l2wr(inp):
    return np.ascontiguousarray(inp["L2_ffn_Wr"])


@hostprep("l2_br")
def _hp_l2br(inp):
    return np.ascontiguousarray(inp["L2_ffn_br"].reshape(1, 8))


@hostprep("l2_W1")
def _hp_l2w1(inp):
    return np.ascontiguousarray(inp["L2_ffn_W1"])


@hostprep("l2_W2")
def _hp_l2w2(inp):
    return np.ascontiguousarray(inp["L2_ffn_W2"])


@hostprep("l2_b1cols")
def _hp_l2b1(inp):
    return np.ascontiguousarray(inp["L2_ffn_b1"].reshape(8, 32, 128).transpose(0, 2, 1))


@hostprep("l2_b2")
def _hp_l2b2(inp):
    return np.ascontiguousarray(inp["L2_ffn_b2"])


@hostprep("l3_Wr")
def _hp_l3wr(inp):
    return np.ascontiguousarray(inp["L3_ffn_centroids"])


@hostprep("l3_W1")
def _hp_l3w1(inp):
    return np.ascontiguousarray(
        np.concatenate([inp["L3_ffn_W1_r"], inp["L3_ffn_W1_s"]], axis=0))


@hostprep("l3_W2")
def _hp_l3w2(inp):
    return np.ascontiguousarray(
        np.concatenate([inp["L3_ffn_W2_r"], inp["L3_ffn_W2_s"]], axis=0))


@hostprep("l3_b1cols")
def _hp_l3b1(inp):
    b = np.concatenate([inp["L3_ffn_b1_r"], inp["L3_ffn_b1_s"]], axis=0)
    return np.ascontiguousarray(b.reshape(10, 32, 128).transpose(0, 2, 1))


@hostprep("l3_b2")
def _hp_l3b2(inp):
    return np.ascontiguousarray(
        np.concatenate([inp["L3_ffn_b2_r"], inp["L3_ffn_b2_s"]], axis=0))


@hostprep("l3_gcr")
def _hp_l3gr(inp):
    return np.ascontiguousarray(inp["L3_ffn_g_routed"].reshape(8, 128).T)


@hostprep("l3_gcs")
def _hp_l3gs(inp):
    return np.ascontiguousarray(inp["L3_ffn_g_shared"].reshape(8, 128).T)


def sub_expert_ffn(c, pfx, which, x_h, x_B, xnT_h, xnT_B, acc_h, accB,
                   xn_tm=None, xnrT=None):
    nc = c.nc
    with ExitStack() as ex:
        w_all = ex.enter_context(nc.sbuf_tensor(pfx + "wall", [128, 8 * NT], F32))
        wT_all = ex.enter_context(nc.sbuf_tensor(pfx + "wTall", [8, T], BF16))
        w_allB = [Buf() for _ in range(NT)]
        wT_allB = Buf()
        if which == "l2":
            rcfg = dict(wr="l2_Wr", br="l2_br", kind="softmax2")
            ecfg = dict(n_routed=8, n_shared=0, W1="l2_W1", W2="l2_W2",
                        b1cols="l2_b1cols", b2="l2_b2")
            exp_in_h, exp_in_B = xnT_h, xnT_B
        else:
            rcfg = dict(wr="l3_Wr", kind="sig2")
            ecfg = dict(n_routed=8, n_shared=2, W1="l3_W1", W2="l3_W2",
                        b1cols="l3_b1cols", b2="l3_b2",
                        gcols_r="l3_gcr", gcols_s="l3_gcs")
            exp_in_h, exp_in_B = xnrT
        st_router(c, pfx + "r_", rcfg, xn_tm[0], xn_tm[1], w_all, w_allB,
                  wT_all, wT_allB)
        st_expert_ffn(c, pfx + "e_", ecfg, exp_in_h, exp_in_B, acc_h, accB,
                      w_all, w_allB, wT_all, wT_allB)
    if which == "l2":
        st_residual_add(c, pfx + "ra_", x_h, x_B, acc_h, accB, 0.5)
    else:
        st_residual_add(c, pfx + "ra_", x_h, x_B, acc_h, accB, 0.5,
                        extra=xn_tm, extra_alpha=0.5)



def st_residual_add(c, pfx, x_h, x_B, acc_h, accB, alpha, extra=None,
                    extra_alpha=0.5):
    nc = c.nc
    with ExitStack() as ex:
        sb = lambda n, s, d: ex.enter_context(nc.sbuf_tensor(pfx + n, s, d))
        xb = [sb("x%d" % i, [128, D], F32) for i in range(2)]
        ab = [sb("a%d" % i, [128, D], F32) for i in range(2)]
        xbB = [Buf() for _ in range(2)]
        abB = [Buf() for _ in range(2)]
        if extra is not None:
            eb = [sb("e%d" % i, [128, D], F32) for i in range(2)]
            ebB = [Buf() for _ in range(2)]
        qs = c.new_dma_sem()
        qa = c.new_dma_sem()
        qo = c.new_dma_sem()
        for i in range(NT):
            j = i % 2
            c.dma("sync", xb[j][:], x_h.ap()[i * 128:(i + 1) * 128, :],
                  reads=(x_B[i],), writes=(xbB[j],), sem=qs)
            c.dma("scalar", ab[j][:], acc_h.ap()[i * 128:(i + 1) * 128, :],
                  reads=(accB[2 * i], accB[2 * i + 1]), writes=(abB[j],), sem=qa)
            c.op("vector", lambda e, j=j: e.scalar_tensor_tensor(
                xb[j][:], ab[j][:], float(alpha), xb[j][:], ALU.mult, ALU.add),
                reads=(abB[j], xbB[j]), writes=(xbB[j],))
            if extra is not None:
                c.dma("scalar", eb[j][:], extra[0].ap()[i * 128:(i + 1) * 128, :],
                      reads=(extra[1],), writes=(ebB[j],), sem=qa)
                c.op("vector", lambda e, j=j: e.scalar_tensor_tensor(
                    xb[j][:], eb[j][:], float(extra_alpha), xb[j][:],
                    ALU.mult, ALU.add),
                    reads=(ebB[j], xbB[j]), writes=(xbB[j],))
            c.dma("sync", x_h.ap()[i * 128:(i + 1) * 128, :], xb[j][:],
                  reads=(xbB[j],), writes=(x_B[i],), sem=qo)
        c.barrier()



@hostprep("l0_rmT")
def _hp_l0rm(inp):
    return np.ascontiguousarray(inp["L0_ffn_rail_maker"].transpose(0, 1, 3, 2))


@hostprep("l0_ruT")
def _hp_l0ru(inp):
    return np.ascontiguousarray(inp["L0_ffn_rail_up_proj"].transpose(0, 1, 3, 2))


@hostprep("l0_cpT")
def _hp_l0cp(inp):
    return np.ascontiguousarray(inp["L0_ffn_w_cproj"].T)


def st_l0_prep(c, pfx, io):
    nc = c.nc
    rm_h, rm_B = c.win("l0_rmT", [4, 64, 16, 16], F32)
    ru_h, ru_B = c.win("l0_ruT", [4, 64, 16, 64], F32)
    m2_h, m2_B = io["l0_m2c"]
    with ExitStack() as ex:
        sb = lambda n, s, d: ex.enter_context(nc.sbuf_tensor(pfx + n, s, d))
        pb = lambda n, s, d: ex.enter_context(nc.psum_tensor(pfx + n, s, d))
        rm = sb("rm", [16, 256 * 16], BF16)
        ru = sb("ru", [16, 256 * 64], BF16)
        m2 = sb("m2", [16, 256 * 64], BF16)
        rmB, ruB, m2B = Buf(), Buf(), Buf()
        ps = [pb("ps%d" % i, [16, 512], F32) for i in range(2)]
        psB = [Buf() for _ in range(2)]
        q = c.new_dma_sem()
        qo = c.new_dma_sem()
        c.dma("gpsimd", rm.rearrange("p (hr x) -> p hr x", x=16),
              rm_h.ap().rearrange("h r j i -> j (h r) i"),
              reads=(rm_B,), writes=(rmB,), sem=q)
        c.dma("gpsimd", ru.rearrange("p (hr x) -> p hr x", x=64),
              ru_h.ap().rearrange("h r j k -> j (h r) k"),
              reads=(ru_B,), writes=(ruB,), sem=q)
        for g in range(32):
            p = g % 2
            c.waits("tensor", (rmB, ruB), (psB[p],))
            inst = None
            for s in range(8):
                hr = g * 8 + s
                inst = nc.tensor.matmul(
                    ps[p][:, s * 64:(s + 1) * 64],
                    rm[:, hr * 16:(hr + 1) * 16],
                    ru[:, hr * 64:(hr + 1) * 64],
                    start=True, stop=True)
                c.stats["instr"] += 1
            c.done("tensor", inst, (rmB, ruB), (psB[p],))
            c.op("vector", lambda e, g=g, p=p: e.tensor_copy(
                m2[:, g * 512:(g + 1) * 512], ps[p][:, :]),
                reads=(psB[p],), writes=(m2B,))
        c.dma("sync", m2_h.ap()[:, :], m2[:, :], reads=(m2B,), writes=(m2_B,),
              sem=qo)
        c.barrier()


def st_l0_ffn(c, pfx, io, x_h, x_B, xnT_h, xnT_B, alpha=0.5):
    nc = c.nc
    HF = 16384
    cp_h, cp_B = c.win("l0_cpT", [HF, D], F32)
    m2_h, m2_B = io["l0_m2c"]
    TC = 256
    NCH = T // TC
    with ExitStack() as ex:
        sb = lambda n, s, d: ex.enter_context(nc.sbuf_tensor(pfx + n, s, d))
        pb = lambda n, s, d: ex.enter_context(nc.psum_tensor(pfx + n, s, d))
        m2 = sb("m2", [16, 256 * 64], BF16)
        m2B = Buf()
        xrail = [sb("xr%d" % i, [16, 64 * TC], BF16) for i in range(2)]
        xrailB = [Buf() for _ in range(2)]
        hb = sb("hb", [128, 32 * TC], BF16)
        hbB = [Buf() for _ in range(32)]
        cpw = [sb("cp%d" % i, [128, D], BF16) for i in range(4)]
        cpwB = [Buf() for _ in range(4)]
        xres = sb("xres", [128, 2 * D], F32)
        xresB = [Buf() for _ in range(2)]
        ev = [sb("ev%d" % i, [128, 512], F32) for i in range(2)]
        evB = [Buf() for _ in range(2)]
        pso = [pb("pso%d" % i, [128, 512], F32) for i in range(4)]
        psoB = [Buf() for _ in range(4)]
        psh = [pb("psh%d" % i, [128, TC], F32) for i in range(2)]
        pshB = [Buf() for _ in range(2)]
        q = c.new_dma_sem()
        qx = c.new_dma_sem()
        qw = c.new_dma_sem()
        qr = c.new_dma_sem()
        qo = c.new_dma_sem()
        c.dma("gpsimd", m2[:, :], m2_h.ap()[:, :], reads=(m2_B,), writes=(m2B,),
              sem=q)
        for ci in range(NCH):
            t0 = ci * TC
            jc = ci % 2
            c.dma("sync",
                  xrail[jc].rearrange("i (r t) -> i r t", r=64),
                  xnT_h.ap()[:, t0:t0 + TC].rearrange("(r i) t -> i r t", i=16),
                  reads=(xnT_B,), writes=(xrailB[jc],), sem=qx)
            for tt in range(2):
                ig = ci * 2 + tt
                c.dma("sync", xres[:, tt * D:(tt + 1) * D],
                      x_h.ap()[ig * 128:(ig + 1) * 128, :],
                      reads=(x_B[ig],), writes=(xresB[tt],), sem=qr)
            for hh in range(4):
                for m in range(32):
                    p = m % 2
                    c.waits("tensor", (m2B, xrailB[jc]), (pshB[p],))
                    inst = None
                    for half in range(2):
                        r = hh * 16 + m // 2
                        rail = hh * 64 + (m * 2 + half)
                        inst = nc.tensor.matmul(
                            psh[p][half * 64:(half + 1) * 64, :],
                            m2[:, rail * 64:(rail + 1) * 64],
                            xrail[jc][:, rail % 64 * TC:(rail % 64) * TC + TC]
                            if False else
                            xrail[jc][:, (m * 2 + half) * TC:(m * 2 + half + 1) * TC],
                            start=True, stop=True)
                        c.stats["instr"] += 1
                    c.done("tensor", inst, (m2B, xrailB[jc]), (pshB[p],))
                    c.op("scalar", lambda e, m=m, p=p: e.activation(
                        out=hb[:, m * TC:(m + 1) * TC], in_=psh[p][:, :],
                        func=AF.Gelu),
                        reads=(pshB[p],), writes=(hbB[m],))
                for m in range(32):
                    k = hh * 32 + m
                    kw = k % 4
                    c.dma("gpsimd", cpw[kw][:], cp_h.ap()[k * 128:(k + 1) * 128, :],
                          reads=(cp_B,), writes=(cpwB[kw],), sem=qw)
                    first = (k == 0)
                    last = (k == 127)
                    c.waits("tensor", (cpwB[kw], hbB[m]),
                            tuple(psoB) if first else ())
                    inst = None
                    for tt in range(2):
                        for dh in range(2):
                            inst = nc.tensor.matmul(
                                pso[tt * 2 + dh][:, :],
                                hb[:, m * TC + tt * 128:m * TC + tt * 128 + 128],
                                cpw[kw][:, dh * 512:(dh + 1) * 512],
                                start=first, stop=last)
                            c.stats["instr"] += 1
                    val = c.done("tensor", inst, (cpwB[kw], hbB[m]), ())
                    if last:
                        for b in psoB:
                            b.w = val
                            b.r = []
            for tt in range(2):
                ig = ci * 2 + tt
                for dh in range(2):
                    je = dh
                    c.op("vector", lambda e, tt=tt, dh=dh, je=je: e.scalar_tensor_tensor(
                        ev[je][:], pso[tt * 2 + dh][:, :], float(alpha),
                        xres[:, tt * D + dh * 512:tt * D + (dh + 1) * 512],
                        ALU.mult, ALU.add),
                        reads=(psoB[tt * 2 + dh], xresB[tt]), writes=(evB[je],))
                    c.dma("sync", x_h.ap()[ig * 128:(ig + 1) * 128,
                                           dh * 512:(dh + 1) * 512],
                          ev[je][:], reads=(evB[je],), writes=(x_B[ig],), sem=qo)
        c.barrier()



@hostprep("l2a_Wq")
def _hp_l2aq(inp):
    return np.ascontiguousarray(inp["L2_attn_Wq"])


@hostprep("l2a_WkT")
def _hp_l2ak(inp):
    return np.ascontiguousarray(inp["L2_attn_Wk"].T)


@hostprep("l2a_WvT")
def _hp_l2av(inp):
    return np.ascontiguousarray(inp["L2_attn_Wv"].T)


@hostprep("l2a_Wl")
def _hp_l2al(inp):
    return np.ascontiguousarray(inp["L2_attn_Wl"])


@hostprep("l2a_Wo")
def _hp_l2ao(inp):
    return np.ascontiguousarray(inp["L2_attn_Wo"])


@hostprep("l2a_bq_cols")
def _hp_l2abq(inp):
    return np.ascontiguousarray(inp["L2_attn_bq"].reshape(8, 128).T)


@hostprep("l2a_bk_col")
def _hp_l2abk(inp):
    return np.ascontiguousarray(inp["L2_attn_bk"].reshape(8, 128).T)


@hostprep("l2a_bv_col")
def _hp_l2abv(inp):
    return np.ascontiguousarray(inp["L2_attn_bv"].reshape(8, 128).T)


@hostprep("l2a_bl_row")
def _hp_l2abl(inp):
    return np.ascontiguousarray(inp["L2_attn_bl"].reshape(1, 1024))


@hostprep("l2a_bo_row")
def _hp_l2abo(inp):
    return np.ascontiguousarray(inp["L2_attn_bo"].reshape(1, 1024))


def st_l2a_prep(c, pfx, io):
    nc = c.nc
    WkT_h, WkT_B = c.win("l2a_WkT", [D, D], F32)
    WvT_h, WvT_B = c.win("l2a_WvT", [D, D], F32)
    Wl_h, Wl_B = c.win("l2a_Wl", [D, D], F32)
    bk_h, bk_B = c.win("l2a_bk_col", [128, 8], F32)
    bv_h, bv_B = c.win("l2a_bv_col", [128, 8], F32)
    bl_h, bl_B = c.win("l2a_bl_row", [1, D], F32)
    WkL_h, WkL_B = io["l2a_WkL"]
    WvL_h, WvL_B = io["l2a_WvL"]
    bkl_h, bkl_B = io["l2a_bkl"]
    bvl_h, bvl_B = io["l2a_bvl"]
    with ExitStack() as ex:
        sb = lambda n, s, d: ex.enter_context(nc.sbuf_tensor(pfx + n, s, d))
        pb = lambda n, s, d: ex.enter_context(nc.psum_tensor(pfx + n, s, d))
        wl = sb("wl", [128, ND * D], BF16)
        wlB = Buf()
        wt = sb("wt", [128, ND * D], BF16)
        wtB = Buf()
        bcol = sb("bcol", [128, 8], BF16)
        bcolB = Buf()
        blrow = sb("blrow", [1, D], F32)
        blrowB = Buf()
        brow = sb("brow", [1, D], F32)
        browB = Buf()
        ev = sb("ev", [128, 512], BF16)
        evB = Buf()
        ps = [pb("ps%d" % i, [128, 512], F32) for i in range(2)]
        psB = [Buf() for _ in range(2)]
        psr = pb("psr", [1, 512], F32)
        psrB = Buf()
        q = c.new_dma_sem()
        qo = c.new_dma_sem()
        for k in range(ND):
            c.dma("gpsimd", wl[:, k * D:(k + 1) * D],
                  Wl_h.ap()[k * 128:(k + 1) * 128, :],
                  reads=(Wl_B,), writes=(wlB,), sem=q)
        c.dma("sync", blrow[:, :], bl_h.ap()[:, :], reads=(bl_B,), writes=(blrowB,),
              sem=q)
        for (wT_h, wT_B, b_h, b_B, out_h, out_B, brow_h, brow_B) in (
                (WkT_h, WkT_B, bk_h, bk_B, WkL_h, WkL_B, bkl_h, bkl_B),
                (WvT_h, WvT_B, bv_h, bv_B, WvL_h, WvL_B, bvl_h, bvl_B)):
            for k in range(ND):
                c.dma("gpsimd", wt[:, k * D:(k + 1) * D],
                      wT_h.ap()[k * 128:(k + 1) * 128, :],
                      reads=(wT_B,), writes=(wtB,), sem=q)
            c.dma("gpsimd", bcol[:, :], b_h.ap()[:, :], reads=(b_B,),
                  writes=(bcolB,), sem=q)
            for m in range(ND):
                for nn in range(2):
                    p = nn % 2
                    c.mm(psB[p], ps[p][:, :],
                         [(wt[:, k * D + m * 128:k * D + (m + 1) * 128],
                           wl[:, k * D + nn * 512:k * D + (nn + 1) * 512])
                          for k in range(ND)],
                         reads=(wtB, wlB))
                    c.op("vector", lambda e, p=p: e.tensor_copy(ev[:], ps[p][:, :]),
                         reads=(psB[p],), writes=(evB,))
                    c.dma("sync", out_h.ap()[m * 128:(m + 1) * 128,
                                             nn * 512:(nn + 1) * 512],
                          ev[:], reads=(evB,), writes=(out_B,), sem=qo)
            for nn in range(2):
                c.mm(psrB, psr[0:1, :],
                     [(bcol[:, k:k + 1],
                       wl[:, k * D + nn * 512:k * D + (nn + 1) * 512])
                      for k in range(ND)],
                     reads=(bcolB, wlB))
                c.op("vector", lambda e, nn=nn: e.tensor_add(
                    brow[0:1, nn * 512:(nn + 1) * 512], psr[0:1, :],
                    blrow[0:1, nn * 512:(nn + 1) * 512]),
                    reads=(psrB, blrowB), writes=(browB,))
            c.dma("sync", brow_h.ap()[:, :], brow[0:1, :], reads=(browB,),
                  writes=(brow_B,), sem=qo)
        c.barrier()


def row_to_cols(c, pfx, ex, row_h, row_B, n):
    nc = c.nc
    cols = ex.enter_context(nc.sbuf_tensor(pfx + "cols", [128, n], F32))
    colsB = Buf()
    q = c.new_dma_sem()
    c.dma("gpsimd", cols[:, :],
          row_h.ap().rearrange("r (k p) -> p k r", p=128),
          reads=(row_B,), writes=(colsB,), sem=q,
          allow_slow_non_contiguous=True)
    return cols, colsB


def st_l2_attn(c, pfx, io, x_h, x_B, xnT_h, xnT_B):
    nc = c.nc
    NH = 16
    HD = 64
    Wq_h, Wq_B = c.win("l2a_Wq", [D, D], F32)
    Wo_h, Wo_B = c.win("l2a_Wo", [D, D], F32)
    bq_h, bq_B = c.win("l2a_bq_cols", [128, 8], F32)
    bo_h, bo_B = c.win("l2a_bo_row", [1, D], F32)
    WkL_h, WkL_B = io["l2a_WkL"]
    WvL_h, WvL_B = io["l2a_WvL"]
    bkl_h, bkl_B = io["l2a_bkl"]
    bvl_h, bvl_B = io["l2a_bvl"]
    qT_h, qT_B = io["l2a_qT"]
    kT_h, kT_B = io["l2a_kT"]
    vT_h, vT_B = io["l2a_vT"]
    with ExitStack() as ex0:
        attnT = ex0.enter_context(nc.sbuf_tensor(pfx + "attnT", [128, ND * T], BF16))
        attnTB = [Buf() for _ in range(ND)]
        with ExitStack() as ex:
            sb = lambda n, s, d: ex.enter_context(nc.sbuf_tensor(pfx + "p" + n, s, d))
            pb = lambda n, s, d: ex.enter_context(nc.psum_tensor(pfx + "p" + n, s, d))
            xnt = sb("xnt", [128, ND * T], BF16)
            xntB = Buf()
            ww = sb("ww", [128, ND * D], BF16)
            wwB = Buf()
            bcol = sb("bcol", [128, 8], F32)
            bcolB = Buf()
            prj = sb("prj", [128, T], BF16)
            prjB = Buf()
            ps = [pb("ps%d" % i, [128, 512], F32) for i in range(2)]
            psB = [Buf() for _ in range(2)]
            q = c.new_dma_sem()
            qw = c.new_dma_sem()
            qo = c.new_dma_sem()
            for k in range(ND):
                c.dma("sync", xnt[:, k * T:(k + 1) * T],
                      xnT_h.ap()[k * 128:(k + 1) * 128, :],
                      reads=(xnT_B,), writes=(xntB,), sem=q)
            for (w_h, w_B, wdt, b_h, b_B, bf32, out_h, out_B, scale) in (
                    (Wq_h, Wq_B, F32, bq_h, bq_B, True, qT_h, qT_B, 0.125),
                    (WkL_h, WkL_B, BF16, bkl_h, bkl_B, False, kT_h, kT_B, 1.0),
                    (WvL_h, WvL_B, BF16, bvl_h, bvl_B, False, vT_h, vT_B, 1.0)):
                for k in range(ND):
                    c.dma("gpsimd", ww[:, k * D:(k + 1) * D],
                          w_h.ap()[k * 128:(k + 1) * 128, :],
                          reads=(w_B,), writes=(wwB,), sem=qw)
                if bf32:
                    c.dma("sync", bcol[:, :], b_h.ap()[:, :], reads=(b_B,),
                          writes=(bcolB,), sem=q)
                else:
                    c.dma("gpsimd", bcol[:, :],
                          b_h.ap().rearrange("r (k p) -> p k r", p=128),
                          reads=(b_B,), writes=(bcolB,), sem=q,
                          allow_slow_non_contiguous=True)
                for m in range(ND):
                    for nn in range(T // 512):
                        p = nn % 2
                        c.mm(psB[p], ps[p][:, :],
                             [(ww[:, k * D + m * 128:k * D + (m + 1) * 128],
                               xnt[:, k * T + nn * 512:k * T + (nn + 1) * 512])
                              for k in range(ND)],
                             reads=(wwB, xntB))
                        c.op("vector", lambda e, p=p, nn=nn, m=m: e.tensor_scalar(
                            prj[:, nn * 512:(nn + 1) * 512], ps[p][:, :],
                            bcol[:, m:m + 1], float(scale), ALU.add, ALU.mult),
                            reads=(psB[p], bcolB), writes=(prjB,))
                    c.dma("sync", out_h.ap()[m * 128:(m + 1) * 128, :], prj[:, :],
                          reads=(prjB,), writes=(out_B,), sem=qo)
            c.barrier()
        with ExitStack() as ex:
            sb = lambda n, s, d: ex.enter_context(nc.sbuf_tensor(pfx + "a" + n, s, d))
            pb = lambda n, s, d: ex.enter_context(nc.psum_tensor(pfx + "a" + n, s, d))
            qh = sb("qh", [64, T], BF16)
            kh = sb("kh", [64, T], BF16)
            vh = sb("vh", [64, T], BF16)
            qhB, khB, vhB = Buf(), Buf(), Buf()
            vhT = sb("vhT", [128, NT * HD], BF16)
            vhTB = Buf()
            Ab = sb("Ab", [128, T], BF16)
            AbB = Buf()
            AT = sb("AT", [128, T], BF16)
            ATB = [Buf() for _ in range(NT)]
            mx = sb("mx", [128, 12], F32)
            mxB = Buf()
            ps = [pb("ps%d" % i, [128, 512], F32) for i in range(8)]
            psB = [Buf() for _ in range(8)]
            q = c.new_dma_sem()
            for h in range(NH):
                hb = (h % 2) * 64
                c.dma("sync", qh[:, :], qT_h.ap()[h * 64:(h + 1) * 64, :],
                      reads=(qT_B,), writes=(qhB,), sem=q)
                c.dma("scalar", kh[:, :], kT_h.ap()[h * 64:(h + 1) * 64, :],
                      reads=(kT_B,), writes=(khB,), sem=q)
                c.dma("gpsimd", vh[:, :], vT_h.ap()[h * 64:(h + 1) * 64, :],
                      reads=(vT_B,), writes=(vhB,), sem=q)
                for s in range(NT):
                    pp = psB[s % 2]
                    c.transpose(pp, ps[s % 2].bitcast(BF16)[:, 0:HD],
                                vh[:, s * 128:(s + 1) * 128], reads=(vhB,),
                                nparts=64)
                    c.op("scalar", lambda e, s=s, p=s % 2: e.copy(
                        vhT[:, s * HD:(s + 1) * HD],
                        ps[p].bitcast(BF16)[:, 0:HD]),
                        reads=(pp,), writes=(vhTB,))
                for i in range(NT):
                    for nn in range(8):
                        c.mm(psB[nn], ps[nn][:, :],
                             [(qh[:, i * 128:(i + 1) * 128],
                               kh[:, nn * 512:(nn + 1) * 512])],
                             reads=(qhB, khB))
                    for nn in range(8):
                        c.op("vector", lambda e, nn=nn: e.reduce_max(
                            mx[:, nn:nn + 1], ps[nn][:, :], axis=AX.X),
                            reads=(psB[nn],), writes=(mxB,))
                    c.op("vector", lambda e: e.reduce_max(mx[:, 8:9], mx[:, 0:8],
                                                          axis=AX.X),
                         reads=(mxB,), writes=(mxB,))
                    c.op("vector", lambda e: e.tensor_scalar(mx[:, 9:10], mx[:, 8:9],
                                                             -1.0, None, ALU.mult),
                         reads=(mxB,), writes=(mxB,))
                    for nn in range(8):
                        c.op("scalar", lambda e, nn=nn: e.activation(
                            out=Ab[:, nn * 512:(nn + 1) * 512], in_=ps[nn][:, :],
                            func=AF.Exp, bias=mx[:, 9:10]),
                            reads=(psB[nn], mxB), writes=(AbB,))
                    c.op("vector", lambda e: e.reduce_sum(mx[:, 10:11], Ab[:, :],
                                                          axis=AX.X),
                         reads=(AbB,), writes=(mxB,))
                    c.op("vector", lambda e: e.reciprocal(mx[:, 11:12], mx[:, 10:11]),
                         reads=(mxB,), writes=(mxB,))
                    for s in range(NT):
                        pp = psB[s % 2]
                        c.transpose(pp, ps[s % 2].bitcast(BF16)[:, 0:128],
                                    Ab[:, s * 128:(s + 1) * 128], reads=(AbB,))
                        c.op("scalar", lambda e, s=s, p=s % 2: e.copy(
                            AT[:, s * 128:(s + 1) * 128],
                            ps[p].bitcast(BF16)[:, 0:128]),
                            reads=(pp,), writes=(ATB[s],))
                    c.waits("tensor", tuple([vhTB] + ATB), (psB[2],))
                    inst = None
                    for s in range(NT):
                        inst = nc.tensor.matmul(
                            ps[2][:, 0:HD], AT[:, s * 128:(s + 1) * 128],
                            vhT[:, s * HD:(s + 1) * HD],
                            start=(s == 0), stop=(s == NT - 1))
                        c.stats["instr"] += 1
                    c.done("tensor", inst, tuple([vhTB] + ATB), (psB[2],))
                    c.op("vector", lambda e: e.tensor_scalar(
                        Ab[:, 256:256 + HD], ps[2][:, 0:HD], mx[:, 11:12],
                        None, ALU.mult),
                        reads=(psB[2], mxB), writes=(AbB,))
                    c.transpose(psB[3], ps[3].bitcast(BF16)[0:HD, 0:128],
                                Ab[:, 256:256 + HD], reads=(AbB,))
                    c.op("vector", lambda e, i=i, h=h, hb=hb: e.tensor_copy(
                        attnT[hb:hb + HD, (h // 2) * T + i * 128:(h // 2) * T + (i + 1) * 128],
                        ps[3].bitcast(BF16)[0:HD, 0:128]),
                        reads=(psB[3],), writes=(attnTB[h // 2],))
            c.barrier()
        if "dbg_attnT" in io:
            dh_, dB_ = io["dbg_attnT"]
            qdbg = c.new_dma_sem()
            for d in range(ND):
                c.dma("sync", dh_.ap()[d * 128:(d + 1) * 128, :],
                      attnT[:, d * T:(d + 1) * T],
                      reads=(attnTB[d],), writes=(dB_,), sem=qdbg)
        with ExitStack() as ex:
            sb = lambda n, s, d: ex.enter_context(nc.sbuf_tensor(pfx + "o" + n, s, d))
            pb = lambda n, s, d: ex.enter_context(nc.psum_tensor(pfx + "o" + n, s, d))
            wo = sb("wo", [128, ND * D], BF16)
            woB = Buf()
            borow = sb("borow", [1, D], BF16)
            borowB = Buf()
            bob = sb("bob", [128, D], F32)
            bobB = Buf()
            xres = sb("xres", [128, D], F32)
            xresB = Buf()
            ev = [sb("ev%d" % i, [128, 512], F32) for i in range(2)]
            evB = [Buf() for _ in range(2)]
            ps = [pb("ps%d" % i, [128, 512], F32) for i in range(3)]
            psB = [Buf() for _ in range(3)]
            q = c.new_dma_sem()
            qo = c.new_dma_sem()
            for k in range(ND):
                c.dma("gpsimd", wo[:, k * D:(k + 1) * D],
                      Wo_h.ap()[k * 128:(k + 1) * 128, :],
                      reads=(Wo_B,), writes=(woB,), sem=q)
            c.dma("gpsimd", borow[:, :], bo_h.ap()[:, :], reads=(bo_B,),
                  writes=(borowB,), sem=q)
            for nn in range(2):
                c.mm(psB[2], ps[2][:, :],
                     [(c.consts["ones"][0:1, 0:128],
                       borow[0:1, nn * 512:(nn + 1) * 512])],
                     reads=(borowB,))
                c.op("vector", lambda e, nn=nn: e.tensor_copy(
                    bob[:, nn * 512:(nn + 1) * 512], ps[2][:, :]),
                    reads=(psB[2],), writes=(bobB,))
            c.barrier()
            if "dbg_wo" in io:
                dwh, dwB = io["dbg_wo"]
                qd2 = c.new_dma_sem()
                for k in range(ND):
                    c.dma("sync", dwh.ap()[k * 128:(k + 1) * 128, :],
                          wo[:, k * D:(k + 1) * D], reads=(woB,), writes=(dwB,),
                          sem=qd2)
                dbh, dbB = io["dbg_bob"]
                c.dma("sync", dbh.ap()[:, :], bob[:, :], reads=(bobB,),
                      writes=(dbB,), sem=qd2)
            for i in range(NT):
                c.dma("sync", xres[:, :], x_h.ap()[i * 128:(i + 1) * 128, :],
                      reads=(x_B[i],), writes=(xresB,), sem=q)
                for nn in range(2):
                    p = nn % 2
                    c.mm(psB[p], ps[p][:, :],
                         [(attnT[:, k * T + i * 128:k * T + (i + 1) * 128],
                           wo[:, k * D + nn * 512:k * D + (nn + 1) * 512])
                          for k in range(ND)],
                         reads=tuple([woB] + attnTB))
                    c.op("vector", lambda e, p=p, nn=nn: e.tensor_add(
                        ev[p][:], ps[p][:, :], bob[:, nn * 512:(nn + 1) * 512]),
                        reads=(psB[p], bobB), writes=(evB[p],))
                    c.op("vector", lambda e, p=p, nn=nn: e.tensor_add(
                        ev[p][:], ev[p][:],
                        xres[:, nn * 512:(nn + 1) * 512]),
                        reads=(evB[p], xresB), writes=(evB[p],))
                    c.dma("sync", x_h.ap()[i * 128:(i + 1) * 128,
                                           nn * 512:(nn + 1) * 512], ev[p][:],
                          reads=(evB[p],), writes=(x_B[i],), sem=qo)
        c.barrier()



def _dft_tables():
    idx = np.arange(1024, dtype=np.float64)
    th = 2.0 * np.pi * np.outer(idx, idx) / 1024.0
    return np.cos(th), np.sin(th)


_DFT_CACHE = {}


def _dft(name):
    if name not in _DFT_CACHE:
        cosm, sinm = _dft_tables()
        _DFT_CACHE["c_dftc"] = cosm.astype(np.float32).astype(bf16)
        _DFT_CACHE["c_dftsn"] = (-sinm).astype(np.float32).astype(bf16)
        _DFT_CACHE["c_idftc"] = (cosm / 1024.0).astype(np.float32).astype(bf16)
        _DFT_CACHE["c_idftsn"] = (-sinm / 1024.0).astype(np.float32).astype(bf16)
    return _DFT_CACHE[name]


for _n in ("c_dftc", "c_dftsn", "c_idftc", "c_idftsn"):
    HOSTPREP[_n] = (lambda nn: (lambda inp: _dft(nn)))(_n)


@hostprep("l0a_Wq")
def _hp_l0aq(inp):
    return np.ascontiguousarray(inp["L0_attn_Wq"])


@hostprep("l0a_Wk")
def _hp_l0ak(inp):
    return np.ascontiguousarray(inp["L0_attn_Wk"])


@hostprep("l0a_Wv")
def _hp_l0av(inp):
    return np.ascontiguousarray(inp["L0_attn_Wv"])


def st_l0a_prep(c, pfx, io):
    nc = c.nc
    dftc_h, dftc_B = c.win("c_dftc", [D, D], BF16)
    dfts_h, dfts_B = c.win("c_dftsn", [D, D], BF16)
    with ExitStack() as ex:
        sb = lambda n, s, d: ex.enter_context(nc.sbuf_tensor(pfx + n, s, d))
        pb = lambda n, s, d: ex.enter_context(nc.psum_tensor(pfx + n, s, d))
        fm = sb("fm", [128, 2 * ND * D], BF16)
        fmB = Buf()
        wq = sb("wq", [128, ND * D], BF16)
        wqB = Buf()
        ev = sb("ev", [128, 512], BF16)
        evB = Buf()
        ps = [pb("ps%d" % i, [128, 512], F32) for i in range(2)]
        psB = [Buf() for _ in range(2)]
        q = c.new_dma_sem()
        qo = c.new_dma_sem()
        for k in range(ND):
            c.dma("gpsimd", fm[:, k * D:(k + 1) * D],
                  dftc_h.ap()[k * 128:(k + 1) * 128, :],
                  reads=(dftc_B,), writes=(fmB,), sem=q)
            c.dma("gpsimd", fm[:, (ND + k) * D:(ND + k + 1) * D],
                  dfts_h.ap()[k * 128:(k + 1) * 128, :],
                  reads=(dfts_B,), writes=(fmB,), sem=q)
        for wname, outs in (("l0a_Wq", ("l0a_fq_re", "l0a_fq_im")),
                            ("l0a_Wk", ("l0a_fk_re", "l0a_fk_im")),
                            ("l0a_Wv", ("l0a_fv_re", "l0a_fv_im"))):
            w_h, w_B = c.win(wname, [D, D], F32)
            for k in range(ND):
                c.dma("gpsimd", wq[:, k * D:(k + 1) * D],
                      w_h.ap()[k * 128:(k + 1) * 128, :],
                      reads=(w_B,), writes=(wqB,), sem=q)
            for ri in range(2):
                out_h, out_B = io[outs[ri]]
                for m in range(ND):
                    for nn in range(2):
                        p = nn % 2
                        c.mm(psB[p], ps[p][:, :],
                             [(wq[:, k * D + m * 128:k * D + (m + 1) * 128],
                               fm[:, (ri * ND + k) * D + nn * 512:
                                  (ri * ND + k) * D + (nn + 1) * 512])
                              for k in range(ND)],
                             reads=(wqB, fmB))
                        c.op("vector", lambda e, p=p: e.tensor_copy(ev[:], ps[p][:, :]),
                             reads=(psB[p],), writes=(evB,))
                        c.dma("sync", out_h.ap()[m * 128:(m + 1) * 128,
                                                 nn * 512:(nn + 1) * 512],
                              ev[:], reads=(evB,), writes=(out_B,), sem=qo)
        c.barrier()


def st_l0_attn(c, pfx, io, x_h, x_B, xnT_h, xnT_B):
    nc = c.nc
    idc_h, idc_B = c.win("c_idftc", [D, D], BF16)
    ids_h, ids_B = c.win("c_idftsn", [D, D], BF16)
    TC = 256
    NCH = T // TC
    with ExitStack() as ex:
        sb = lambda n, s, d: ex.enter_context(nc.sbuf_tensor(pfx + n, s, d))
        pb = lambda n, s, d: ex.enter_context(nc.psum_tensor(pfx + n, s, d))
        idft = sb("idft", [128, 2 * ND * D], BF16)
        idftB = Buf()
        wfold = [sb("wf%d" % i, [128, ND * D], BF16) for i in range(2)]
        wfoldB = [Buf() for _ in range(2)]
        xc = [sb("xc%d" % i, [128, ND * TC], BF16) for i in range(2)]
        xcB = [Buf() for _ in range(2)]
        fq = sb("fq", [128, 2 * ND * TC], F32)
        fqB = Buf()
        fk = sb("fk", [128, 2 * ND * TC], F32)
        fkB = Buf()
        fv = sb("fv", [128, 2 * ND * TC], F32)
        fvB = Buf()
        pr = sb("pr", [128, 2 * ND * TC], F32)
        prB = [Buf() for _ in range(2 * ND)]
        tmp = sb("tmp", [128, TC], F32)
        tmpB = Buf()
        gg = sb("gg", [128, 2 * ND * TC], BF16)
        ggB = [Buf() for _ in range(2 * ND)]
        carry = sb("carry", [128, 2 * ND], F32)
        carryB = [Buf() for _ in range(2 * ND)]
        ot = sb("ot", [128, ND * TC], BF16)
        otB = [Buf() for _ in range(ND)]
        xres = [sb("xres%d" % i, [128, D], F32) for i in range(2)]
        xresB = [Buf() for _ in range(2)]
        ev = [sb("ev%d" % i, [128, D], F32) for i in range(2)]
        evB = [Buf() for _ in range(2)]
        ps = [pb("ps%d" % i, [128, 512], F32) for i in range(4)]
        psB = [Buf() for _ in range(4)]
        pst = [pb("pst%d" % i, [128, 128], BF16) for i in range(2)]
        pstB = [Buf() for _ in range(2)]
        q = c.new_dma_sem()
        qw = c.new_dma_sem()
        qx = c.new_dma_sem()
        qr = c.new_dma_sem()
        qo = c.new_dma_sem()
        for k in range(ND):
            c.dma("gpsimd", idft[:, k * D:(k + 1) * D],
                  idc_h.ap()[k * 128:(k + 1) * 128, :],
                  reads=(idc_B,), writes=(idftB,), sem=q)
            c.dma("gpsimd", idft[:, (ND + k) * D:(ND + k + 1) * D],
                  ids_h.ap()[k * 128:(k + 1) * 128, :],
                  reads=(ids_B,), writes=(idftB,), sem=q)
        for m in range(2 * ND):
            c.op("vector", lambda e, m=m: e.memset(carry[:, m:m + 1], 0.0),
                 writes=(carryB[m],))
        names = [("l0a_fq_re", "l0a_fq_im", fq, fqB),
                 ("l0a_fk_re", "l0a_fk_im", fk, fkB),
                 ("l0a_fv_re", "l0a_fv_im", fv, fvB)]
        for ci in range(NCH):
            t0 = ci * TC
            jc = ci % 2
            for k in range(ND):
                c.dma("sync", xc[jc][:, k * TC:(k + 1) * TC],
                      xnT_h.ap()[k * 128:(k + 1) * 128, t0:t0 + TC],
                      reads=(xnT_B,), writes=(xcB[jc],), sem=qx)
            for wre, wim, dst, dstB in names:
                for ri, wname in enumerate((wre, wim)):
                    w_h, w_B = io[wname]
                    jw = ri % 2
                    for k in range(ND):
                        c.dma("gpsimd", wfold[jw][:, k * D:(k + 1) * D],
                              w_h.ap()[k * 128:(k + 1) * 128, :],
                              reads=(w_B,), writes=(wfoldB[jw],), sem=qw)
                    for m in range(ND):
                        pk = (ri * 2 + m) % 4
                        c.mm(psB[pk], ps[pk][:, 0:TC],
                             [(wfold[jw][:, k * D + m * 128:k * D + (m + 1) * 128],
                               xc[jc][:, k * TC:(k + 1) * TC]) for k in range(ND)],
                             reads=(wfoldB[jw], xcB[jc]))
                        c.op("scalar", lambda e, m=m, ri=ri, pk=pk, dst=dst: e.copy(
                            dst[:, (ri * ND + m) * TC:(ri * ND + m + 1) * TC],
                            ps[pk][:, 0:TC]),
                            reads=(psB[pk],), writes=(dstB,))
            for m in range(ND):
                mre, mim = m, ND + m
                s_re = slice(mre * TC, (mre + 1) * TC)
                s_im = slice(mim * TC, (mim + 1) * TC)
                c.op("vector", lambda e: e.tensor_mul(
                    pr[:, s_re], fk[:, s_re], fv[:, s_re]),
                    reads=(fkB, fvB), writes=(prB[mre],))
                c.op("vector", lambda e: e.tensor_mul(tmp[:], fk[:, s_im],
                                                      fv[:, s_im]),
                    reads=(fkB, fvB), writes=(tmpB,))
                c.op("vector", lambda e: e.tensor_sub(pr[:, s_re], pr[:, s_re],
                                                      tmp[:]),
                    reads=(tmpB,), writes=(prB[mre],))
                c.op("vector", lambda e: e.tensor_mul(
                    pr[:, s_im], fk[:, s_re], fv[:, s_im]),
                    reads=(fkB, fvB), writes=(prB[mim],))
                c.op("vector", lambda e: e.tensor_mul(tmp[:], fk[:, s_im],
                                                      fv[:, s_re]),
                    reads=(fkB, fvB), writes=(tmpB,))
                c.op("vector", lambda e: e.tensor_add(pr[:, s_im], pr[:, s_im],
                                                      tmp[:]),
                    reads=(tmpB,), writes=(prB[mim],))
                for mm in (mre, mim):
                    sl = slice(mm * TC, (mm + 1) * TC)
                    c.op("vector", lambda e, mm=mm, sl=sl: e.tensor_tensor_scan(
                        pr[:, sl], pr[:, sl], pr[:, sl], carry[:, mm:mm + 1],
                        ALU.add, ALU.bypass),
                        reads=(carryB[mm],), writes=(prB[mm],))
                    c.op("vector", lambda e, mm=mm, sl=sl: e.tensor_copy(
                        carry[:, mm:mm + 1], pr[:, mm * TC + TC - 1:mm * TC + TC]),
                        reads=(prB[mm],), writes=(carryB[mm],))
                c.op("vector", lambda e: e.tensor_mul(tmp[:], pr[:, s_im],
                                                      fq[:, s_im]),
                    reads=(prB[mim], fqB), writes=(tmpB,))
                c.op("vector", lambda e: e.scalar_tensor_tensor(
                    gg[:, s_re], pr[:, s_re], 1.0, fq[:, s_re], ALU.mult,
                    ALU.mult),
                    reads=(prB[mre], fqB), writes=(ggB[mre],))
                c.op("vector", lambda e: e.tensor_add(gg[:, s_re],
                                                      gg[:, s_re], tmp[:]),
                    reads=(tmpB,), writes=(ggB[mre],))
                c.op("vector", lambda e: e.tensor_mul(tmp[:], pr[:, s_re],
                                                      fq[:, s_im]),
                    reads=(prB[mre], fqB), writes=(tmpB,))
                c.op("vector", lambda e: e.scalar_tensor_tensor(
                    gg[:, s_im], pr[:, s_im], 1.0, fq[:, s_re], ALU.mult,
                    ALU.mult),
                    reads=(prB[mim], fqB), writes=(ggB[mim],))
                c.op("vector", lambda e: e.tensor_sub(gg[:, s_im],
                                                      gg[:, s_im], tmp[:]),
                    reads=(tmpB,), writes=(ggB[mim],))
            for m in range(ND):
                pk = m % 2
                pairs = []
                for k in range(ND):
                    pairs.append((idft[:, k * D + m * 128:k * D + (m + 1) * 128],
                                  gg[:, k * TC:(k + 1) * TC]))
                    pairs.append((idft[:, (ND + k) * D + m * 128:
                                       (ND + k) * D + (m + 1) * 128],
                                  gg[:, (ND + k) * TC:(ND + k + 1) * TC]))
                c.mm(psB[pk], ps[pk][:, 0:TC], pairs,
                     reads=tuple([idftB] + ggB))
                c.op("scalar", lambda e, m=m, pk=pk: e.copy(
                    ot[:, m * TC:(m + 1) * TC], ps[pk][:, 0:TC]),
                    reads=(psB[pk],), writes=(otB[m],))
            for tt in range(TC // 128):
                ig = (t0 + tt * 128) // 128
                je = tt % 2
                c.dma("sync", xres[je][:], x_h.ap()[ig * 128:(ig + 1) * 128, :],
                      reads=(x_B[ig],), writes=(xresB[je],), sem=qr)
                for m in range(ND):
                    pk = m % 2
                    c.transpose(pstB[pk], pst[pk][:, :],
                                ot[:, m * TC + tt * 128:m * TC + (tt + 1) * 128],
                                reads=(otB[m],))
                    c.op("vector", lambda e, m=m, pk=pk, je=je: e.tensor_copy(
                        ev[je][:, m * 128:(m + 1) * 128], pst[pk][:, :]),
                        reads=(pstB[pk],), writes=(evB[je],))
                c.op("vector", lambda e, je=je: e.tensor_add(ev[je][:], ev[je][:],
                                                             xres[je][:]),
                     reads=(xresB[je],), writes=(evB[je],))
                c.dma("sync", x_h.ap()[ig * 128:(ig + 1) * 128, :], ev[je][:],
                      reads=(evB[je],), writes=(x_B[ig],), sem=qo)
        c.barrier()



L1_LEVELS = []
_o = 0
_L = 4
while _L <= T:
    L1_LEVELS.append((_L, _o, T // _L))
    _o += T // _L
    _L *= 2
L1_NSETS = _o


@hostprep("l1a_WaT")
def _hp_l1aw(inp):
    return np.ascontiguousarray(inp["L1_attn_W_attn"].T)


@hostprep("l1a_WpT")
def _hp_l1ap(inp):
    return np.ascontiguousarray(inp["L1_attn_W_proj"].T)


@hostprep("l1a_ridx")
def _hp_l1ar(inp):
    r = np.zeros((1, L1_NSETS), np.float32)
    for (L, o, cnt) in L1_LEVELS:
        r[0, o:o + cnt] = np.arange(1, cnt + 1, dtype=np.float32) * L - 1
    return r


@hostprep("c_iota128")
def _hp_iota(inp):
    return np.arange(128, dtype=np.float32).reshape(128, 1)


def st_l1_attn(c, pfx, io, x_h, x_B, xnT_h, xnT_B):
    nc = c.nc
    NH = 16
    HD = 64
    NS = 2048
    WaT_h, WaT_B = c.win("l1a_WaT", [D, 3 * D], F32)
    WpT_h, WpT_B = c.win("l1a_WpT", [D, D], F32)
    ridx_h, ridx_B = c.win("l1a_ridx", [1, L1_NSETS], F32)
    iota_h, iota_B = c.win("c_iota128", [128, 1], F32)
    qT_h, qT_B = io["l1a_qT"]
    kT_h, kT_B = io["l1a_kT"]
    vT_h, vT_B = io["l1a_vT"]
    with ExitStack() as ex0:
        attnT = ex0.enter_context(nc.sbuf_tensor(pfx + "attnT", [128, ND * T], BF16))
        attnTB = [Buf() for _ in range(ND)]
        rbc = ex0.enter_context(nc.sbuf_tensor(pfx + "rbc", [128, L1_NSETS], F32))
        rbcB = Buf()
        iota = ex0.enter_context(nc.sbuf_tensor(pfx + "iota", [128, 1], F32))
        iotaB = Buf()
        with ExitStack() as ex:
            sb = lambda n, s, d: ex.enter_context(nc.sbuf_tensor(pfx + "p" + n, s, d))
            pb = lambda n, s, d: ex.enter_context(nc.psum_tensor(pfx + "p" + n, s, d))
            xnt = sb("xnt", [128, ND * T], BF16)
            xntB = Buf()
            ww = sb("ww", [128, ND * D], BF16)
            wwB = Buf()
            prj = sb("prj", [128, T], BF16)
            prjB = Buf()
            rrow = sb("rrow", [1, L1_NSETS], BF16)
            rrowB = Buf()
            ps = [pb("ps%d" % i, [128, 512], F32) for i in range(2)]
            psB = [Buf() for _ in range(2)]
            psb = pb("psb", [128, 512], F32)
            psbB = Buf()
            q = c.new_dma_sem()
            qw = c.new_dma_sem()
            qo = c.new_dma_sem()
            c.dma("sync", iota[:, :], iota_h.ap()[:, :], reads=(iota_B,),
                  writes=(iotaB,), sem=q)
            c.dma("gpsimd", rrow[:, :], ridx_h.ap()[:, :], reads=(ridx_B,),
                  writes=(rrowB,), sem=q)
            for nn in range(4):
                n0 = nn * 512
                n1 = min(L1_NSETS, n0 + 512)
                c.mm(psbB, psb[:, 0:n1 - n0],
                     [(c.consts["ones"][0:1, 0:128], rrow[0:1, n0:n1])],
                     reads=(rrowB,))
                c.op("vector", lambda e, n0=n0, n1=n1: e.tensor_copy(
                    rbc[:, n0:n1], psb[:, 0:n1 - n0]),
                    reads=(psbB,), writes=(rbcB,))
            for k in range(ND):
                c.dma("sync", xnt[:, k * T:(k + 1) * T],
                      xnT_h.ap()[k * 128:(k + 1) * 128, :],
                      reads=(xnT_B,), writes=(xntB,), sem=q)
            for pi, (out_h, out_B, scale) in enumerate(
                    ((qT_h, qT_B, 0.125), (kT_h, kT_B, 1.0), (vT_h, vT_B, 1.0))):
                for k in range(ND):
                    c.dma("gpsimd", ww[:, k * D:(k + 1) * D],
                          WaT_h.ap()[k * 128:(k + 1) * 128, pi * D:(pi + 1) * D],
                          reads=(WaT_B,), writes=(wwB,), sem=qw)
                for m in range(ND):
                    for nn in range(T // 512):
                        p = nn % 2
                        c.mm(psB[p], ps[p][:, :],
                             [(ww[:, k * D + m * 128:k * D + (m + 1) * 128],
                               xnt[:, k * T + nn * 512:k * T + (nn + 1) * 512])
                              for k in range(ND)],
                             reads=(wwB, xntB))
                        c.op("vector", lambda e, p=p, nn=nn: e.tensor_scalar(
                            prj[:, nn * 512:(nn + 1) * 512], ps[p][:, :],
                            float(scale), None, ALU.mult),
                            reads=(psB[p],), writes=(prjB,))
                    c.dma("sync", out_h.ap()[m * 128:(m + 1) * 128, :], prj[:, :],
                          reads=(prjB,), writes=(out_B,), sem=qo)
            c.barrier()
        with ExitStack() as ex:
            sb = lambda n, s, d: ex.enter_context(nc.sbuf_tensor(pfx + "a" + n, s, d))
            pb = lambda n, s, d: ex.enter_context(nc.psum_tensor(pfx + "a" + n, s, d))
            qh = sb("qh", [64, T], BF16)
            kh = sb("kh", [64, T], BF16)
            vh = sb("vh", [64, T], BF16)
            qhB, khB, vhB = Buf(), Buf(), Buf()
            kset = sb("kset", [64, NS], F32)
            vset = sb("vset", [64, NS], F32)
            ksetB, vsetB = Buf(), Buf()
            ksbf = sb("ksbf", [64, NS], BF16)
            vsbf = sb("vsbf", [64, NS], BF16)
            ksbfB, vsbfB = Buf(), Buf()
            vsT = sb("vsT", [128, (NS // 128) * HD], BF16)
            vsTB = Buf()
            ktl = sb("ktl", [64, T], F32)
            vtl = sb("vtl", [64, T], F32)
            ktlB, vtlB = Buf(), Buf()
            vtbf = sb("vtbf", [64, T], BF16)
            vtbfB = Buf()
            tl = sb("tl", [64, T], BF16)
            tlB = Buf()
            trow = sb("trow", [1, T], BF16)
            trowB = Buf()
            tcol = sb("tcol", [128, 1], F32)
            tcolB = Buf()
            Sb = sb("Sb", [128, NS], F32)
            SbB = Buf()
            Abf = sb("Abf", [128, NS], BF16)
            AbfB = Buf()
            AT = sb("AT", [128, NS], BF16)
            ATB = [Buf() for _ in range(NS // 128)]
            mx = sb("mx", [128, 4], F32)
            mxB = Buf()
            msk = sb("msk", [128, L1_NSETS], F32)
            mskB = Buf()
            ev1 = sb("ev1", [128, HD], F32)
            ev1B = Buf()
            ev2 = sb("ev2", [128, HD], BF16)
            ev2B = Buf()
            ps = [pb("ps%d" % i, [128, 512], F32) for i in range(8)]
            psB = [Buf() for _ in range(8)]
            q = c.new_dma_sem()
            for h in range(NH):
                hb = (h % 2) * 64
                c.dma("sync", qh[:, :], qT_h.ap()[h * 64:(h + 1) * 64, :],
                      reads=(qT_B,), writes=(qhB,), sem=q)
                c.dma("scalar", kh[:, :], kT_h.ap()[h * 64:(h + 1) * 64, :],
                      reads=(kT_B,), writes=(khB,), sem=q)
                c.dma("gpsimd", vh[:, :], vT_h.ap()[h * 64:(h + 1) * 64, :],
                      reads=(vT_B,), writes=(vhB,), sem=q)
                for (src, srcB, dst, dstB) in ((kh, khB, kset, ksetB),
                                               (vh, vhB, vset, vsetB)):
                    c.op("vector", lambda e, src=src, dst=dst: e.tensor_add(
                        dst[:, 0:1024], src[:, 0::4], src[:, 1::4]),
                        reads=(srcB,), writes=(dstB,))
                    c.op("vector", lambda e, src=src, dst=dst: e.tensor_add(
                        tl[:, 0:1024], src[:, 2::4], src[:, 3::4]),
                        reads=(srcB,), writes=(tlB,))
                    c.op("vector", lambda e, dst=dst: e.tensor_add(
                        dst[:, 0:1024], dst[:, 0:1024], tl[:, 0:1024]),
                        reads=(tlB,), writes=(dstB,))
                    for (L, o, cnt) in L1_LEVELS[1:]:
                        po = L1_LEVELS[[lv[0] for lv in L1_LEVELS].index(L // 2)][1]
                        c.op("vector", lambda e, dst=dst, o=o, cnt=cnt, po=po:
                             e.tensor_add(dst[:, o:o + cnt],
                                          dst[:, po:po + 2 * cnt:2],
                                          dst[:, po + 1:po + 2 * cnt:2]),
                             reads=(dstB,), writes=(dstB,))
                    c.op("vector", lambda e, dst=dst: e.memset(
                        dst[:, L1_NSETS:NS], 0.0), writes=(dstB,))
                c.op("vector", lambda e: e.tensor_copy(ksbf[:, :], kset[:, :]),
                     reads=(ksetB,), writes=(ksbfB,))
                c.op("vector", lambda e: e.tensor_copy(vsbf[:, :], vset[:, :]),
                     reads=(vsetB,), writes=(vsbfB,))
                for s in range(NS // 128):
                    pp = psB[s % 2]
                    c.transpose(pp, ps[s % 2].bitcast(BF16)[:, 0:HD],
                                vsbf[:, s * 128:(s + 1) * 128], reads=(vsbfB,),
                                nparts=64)
                    c.op("vector", lambda e, s=s, p=s % 2: e.tensor_copy(
                        vsT[:, s * HD:(s + 1) * HD], ps[p].bitcast(BF16)[:, 0:HD]),
                        reads=(pp,), writes=(vsTB,))
                for (src, srcB, dst, dstB) in ((kh, khB, ktl, ktlB),
                                               (vh, vhB, vtl, vtlB)):
                    c.op("vector", lambda e, src=src, dst=dst: e.tensor_copy(
                        dst[:, 0::4], src[:, 0::4]), reads=(srcB,), writes=(dstB,))
                    for p in range(1, 4):
                        c.op("vector", lambda e, src=src, dst=dst, p=p: e.tensor_add(
                            dst[:, p::4], dst[:, p - 1::4], src[:, p::4]),
                            reads=(srcB,), writes=(dstB,))
                c.op("vector", lambda e: e.tensor_copy(vtbf[:, :], vtl[:, :]),
                     reads=(vtlB,), writes=(vtbfB,))
                c.op("vector", lambda e: e.tensor_mul(tl[:, :], qh[:, :], ktl[:, :]),
                     reads=(qhB, ktlB), writes=(tlB,))
                for nn in range(8):
                    c.mm(psB[4 + nn % 2], ps[4 + nn % 2][0:1, :],
                         [(c.consts["onescol"][0:64, 0:1],
                           tl[:, nn * 512:(nn + 1) * 512])],
                         reads=(tlB,))
                    c.op("vector", lambda e, nn=nn: e.tensor_copy(
                        trow[0:1, nn * 512:(nn + 1) * 512],
                        ps[4 + nn % 2][0:1, :]),
                        reads=(psB[4 + nn % 2],), writes=(trowB,))
                if h == 0 and "dbg_kset" in io:
                    dkh, dkB = io["dbg_kset"]
                    qdk = c.new_dma_sem()
                    c.dma("gpsimd", dkh.ap()[:, :], kset[:, :], reads=(ksetB,),
                          writes=(dkB,), sem=qdk)
                    dth, dtB = io["dbg_ktl"]
                    c.dma("gpsimd", dth.ap()[:, :], ktl[:, :], reads=(ktlB,),
                          writes=(dtB,), sem=qdk)
                    dvh, dvB = io["dbg_vset"]
                    c.dma("gpsimd", dvh.ap()[:, :], vset[:, :], reads=(vsetB,),
                          writes=(dvB,), sem=qdk)
                    dwh, dwB = io["dbg_vtl"]
                    c.dma("gpsimd", dwh.ap()[:, :], vtl[:, :], reads=(vtlB,),
                          writes=(dwB,), sem=qdk)
                for i in range(NT):
                    for nn in range(4):
                        n0 = nn * 512
                        n1 = min(L1_NSETS, n0 + 512)
                        c.mm(psB[nn], ps[nn][:, 0:n1 - n0],
                             [(qh[:, i * 128:(i + 1) * 128], ksbf[:, n0:n1])],
                             reads=(qhB, ksbfB))
                        c.op("vector", lambda e, nn=nn, n0=n0, n1=n1: e.tensor_copy(
                            Sb[:, n0:n1], ps[nn][:, 0:n1 - n0]),
                            reads=(psB[nn],), writes=(SbB,))
                    c.transpose(psB[4], ps[4].bitcast(BF16)[:, 0:1],
                                trow[0:1, i * 128:(i + 1) * 128], reads=(trowB,),
                                nparts=1)
                    c.op("vector", lambda e: e.tensor_copy(
                        Sb[:, L1_NSETS:NS], ps[4].bitcast(BF16)[:, 0:1]),
                        reads=(psB[4],), writes=(SbB,))
                    c.op("vector", lambda e, i=i: e.tensor_scalar(
                        tcol[:, :], iota[:, :], float(128 * i), None, ALU.add),
                        reads=(iotaB,), writes=(tcolB,))
                    c.op("vector", lambda e: e.tensor_scalar(
                        msk[:, :], rbc[:, :], tcol[:, 0:1], None, ALU.is_le),
                        reads=(rbcB, tcolB), writes=(mskB,))
                    c.op("vector", lambda e: e.tensor_mul(
                        Sb[:, 0:L1_NSETS], Sb[:, 0:L1_NSETS], msk[:, :]),
                        reads=(mskB,), writes=(SbB,))
                    c.op("vector", lambda e: e.reduce_max(mx[:, 0:1], Sb[:, :],
                                                          axis=AX.X, negate=True),
                         reads=(SbB,), writes=(mxB,))
                    c.op("scalar", lambda e: e.activation(out=Sb[:, :], in_=Sb[:, :],
                                                          func=AF.Exp,
                                                          bias=mx[:, 0:1]),
                         reads=(mxB,), writes=(SbB,))
                    c.op("vector", lambda e: e.tensor_mul(
                        Sb[:, 0:L1_NSETS], Sb[:, 0:L1_NSETS], msk[:, :]),
                        reads=(mskB,), writes=(SbB,))
                    c.op("vector", lambda e: e.reduce_sum(mx[:, 1:2], Sb[:, :],
                                                          axis=AX.X),
                         reads=(SbB,), writes=(mxB,))
                    c.op("vector", lambda e: e.reciprocal(mx[:, 2:3], mx[:, 1:2]),
                         reads=(mxB,), writes=(mxB,))
                    c.op("vector", lambda e: e.tensor_scalar(Abf[:, :], Sb[:, :],
                                                             mx[:, 2:3], None,
                                                             ALU.mult),
                         reads=(SbB, mxB), writes=(AbfB,))
                    for s in range(NS // 128):
                        pp = psB[5 + s % 2]
                        c.transpose(pp, ps[5 + s % 2].bitcast(BF16)[:, 0:128],
                                    Abf[:, s * 128:(s + 1) * 128], reads=(AbfB,))
                        c.op("scalar", lambda e, s=s, p=5 + s % 2: e.copy(
                            AT[:, s * 128:(s + 1) * 128],
                            ps[p].bitcast(BF16)[:, 0:128]),
                            reads=(pp,), writes=(ATB[s],))
                    c.waits("tensor", tuple([vsTB] + ATB), (psB[7],))
                    inst = None
                    for s in range(NS // 128):
                        inst = nc.tensor.matmul(
                            ps[7][:, 0:HD], AT[:, s * 128:(s + 1) * 128],
                            vsT[:, s * HD:(s + 1) * HD],
                            start=(s == 0), stop=(s == NS // 128 - 1))
                        c.stats["instr"] += 1
                    c.done("tensor", inst, tuple([vsTB] + ATB), (psB[7],))
                    c.op("vector", lambda e: e.tensor_copy(ev1[:, :],
                                                           ps[7][:, 0:HD]),
                         reads=(psB[7],), writes=(ev1B,))
                    c.transpose(psB[4], ps[4].bitcast(BF16)[:, 0:HD],
                                vtbf[:, i * 128:(i + 1) * 128], reads=(vtbfB,),
                                nparts=64)
                    c.op("vector", lambda e: e.scalar_tensor_tensor(
                        ev1[:, :], ps[4].bitcast(BF16)[:, 0:HD],
                        Abf[:, L1_NSETS:NS], ev1[:, :], ALU.mult, ALU.add),
                        reads=(psB[4], AbfB, ev1B), writes=(ev1B,))
                    c.op("vector", lambda e: e.tensor_copy(ev2[:, :], ev1[:, :]),
                         reads=(ev1B,), writes=(ev2B,))
                    c.transpose(psB[6], ps[6].bitcast(BF16)[0:HD, 0:128],
                                ev2[:, :], reads=(ev2B,))
                    c.op("vector", lambda e, i=i, h=h, hb=hb: e.tensor_copy(
                        attnT[hb:hb + HD,
                              (h // 2) * T + i * 128:(h // 2) * T + (i + 1) * 128],
                        ps[6].bitcast(BF16)[0:HD, 0:128]),
                        reads=(psB[6],), writes=(attnTB[h // 2],))
            c.barrier()
        if "dbg_attnT" in io:
            dh_, dB_ = io["dbg_attnT"]
            qdbg = c.new_dma_sem()
            for d in range(ND):
                c.dma("sync", dh_.ap()[d * 128:(d + 1) * 128, :],
                      attnT[:, d * T:(d + 1) * T],
                      reads=(attnTB[d],), writes=(dB_,), sem=qdbg)
        with ExitStack() as ex:
            sb = lambda n, s, d: ex.enter_context(nc.sbuf_tensor(pfx + "o" + n, s, d))
            pb = lambda n, s, d: ex.enter_context(nc.psum_tensor(pfx + "o" + n, s, d))
            wo = sb("wo", [128, ND * D], BF16)
            woB = Buf()
            xres = sb("xres", [128, D], F32)
            xresB = Buf()
            ev = [sb("ev%d" % i, [128, 512], F32) for i in range(2)]
            evB = [Buf() for _ in range(2)]
            ps = [pb("ps%d" % i, [128, 512], F32) for i in range(2)]
            psB = [Buf() for _ in range(2)]
            q = c.new_dma_sem()
            qo = c.new_dma_sem()
            for k in range(ND):
                c.dma("gpsimd", wo[:, k * D:(k + 1) * D],
                      WpT_h.ap()[k * 128:(k + 1) * 128, :],
                      reads=(WpT_B,), writes=(woB,), sem=q)
            c.barrier()
            for i in range(NT):
                c.dma("sync", xres[:, :], x_h.ap()[i * 128:(i + 1) * 128, :],
                      reads=(x_B[i],), writes=(xresB,), sem=q)
                for nn in range(2):
                    p = nn % 2
                    c.mm(psB[p], ps[p][:, :],
                         [(attnT[:, k * T + i * 128:k * T + (i + 1) * 128],
                           wo[:, k * D + nn * 512:k * D + (nn + 1) * 512])
                          for k in range(ND)],
                         reads=tuple([woB] + attnTB))
                    c.op("vector", lambda e, p=p, nn=nn: e.tensor_add(
                        ev[p][:], ps[p][:, :], xres[:, nn * 512:(nn + 1) * 512]),
                        reads=(psB[p], xresB), writes=(evB[p],))
                    c.dma("sync", x_h.ap()[i * 128:(i + 1) * 128,
                                           nn * 512:(nn + 1) * 512], ev[p][:],
                          reads=(evB[p],), writes=(x_B[i],), sem=qo)
            c.barrier()



@hostprep("c_tri128")
def _hp_tri(inp):
    s = np.arange(128)[:, None]
    t = np.arange(128)[None, :]
    return np.where(s <= t, 0.0, -1e30).astype(np.float32)


def _l3a_hp(sfx):
    HOSTPREP["l3a_WiT_" + sfx] = (lambda s: lambda inp: np.ascontiguousarray(
        inp["L3_attn_Wi_" + s].T))(sfx)
    HOSTPREP["l3a_cw_" + sfx] = (lambda s: lambda inp: np.ascontiguousarray(
        np.concatenate([inp["L3_attn_conv_w_" + s][:, 0, j].reshape(18, 128).T
                        for j in range(4)], axis=1)))(sfx)
    HOSTPREP["l3a_cb_" + sfx] = (lambda s: lambda inp: np.ascontiguousarray(
        inp["L3_attn_conv_b_" + s].reshape(18, 128).T))(sfx)
    HOSTPREP["l3a_dtb_" + sfx] = (lambda s: lambda inp: np.ascontiguousarray(
        inp["L3_attn_dt_bias_" + s].reshape(32, 1)))(sfx)
    HOSTPREP["l3a_Alog_" + sfx] = (lambda s: lambda inp: np.ascontiguousarray(
        inp["L3_attn_A_log_" + s].reshape(32, 1)))(sfx)
    HOSTPREP["l3a_D_" + sfx] = (lambda s: lambda inp: np.ascontiguousarray(
        inp["L3_attn_D_" + s].reshape(32, 1)))(sfx)
    HOSTPREP["l3a_WoT_" + sfx] = (lambda s: lambda inp: np.ascontiguousarray(
        inp["L3_attn_Wo_" + s].T))(sfx)
    HOSTPREP["l3a_nw_" + sfx] = (lambda s: lambda inp: np.ascontiguousarray(
        inp["L3_attn_norm_w_" + s].reshape(16, 128).T))(sfx)


_l3a_hp("f")
_l3a_hp("b")


def st_l3a_prep(c, pfx, io):
    nc = c.nc
    with ExitStack() as ex:
        sb = lambda n, s, d: ex.enter_context(nc.sbuf_tensor(pfx + n, s, d))
        w = sb("w", [128, D], F32)
        wb = sb("wb", [128, D], BF16)
        nw = sb("nw", [128, 16], F32)
        wB, wbB, nwB = Buf(), Buf(), Buf()
        q = c.new_dma_sem()
        qo = c.new_dma_sem()
        for sfx in ("f", "b"):
            WoT_h, WoT_B = c.win("l3a_WoT_" + sfx, [2048, D], F32)
            nw_h, nw_B = c.win("l3a_nw_" + sfx, [128, 16], F32)
            out_h, out_B = io["l3a_WoTn_" + sfx]
            c.dma("sync", nw[:, :], nw_h.ap()[:, :], reads=(nw_B,), writes=(nwB,),
                  sem=q)
            for k in range(16):
                c.dma("sync", w[:, :], WoT_h.ap()[k * 128:(k + 1) * 128, :],
                      reads=(WoT_B,), writes=(wB,), sem=q)
                c.op("vector", lambda e, k=k: e.tensor_scalar(
                    wb[:, :], w[:, :], nw[:, k:k + 1], None, ALU.mult),
                    reads=(wB, nwB), writes=(wbB,))
                c.dma("sync", out_h.ap()[k * 128:(k + 1) * 128, :], wb[:, :],
                      reads=(wbB,), writes=(out_B,), sem=qo)
        c.barrier()


def st_l3a_reverse(c, pfx, xnT_h, xnT_B, xnTr_h, xnTr_B):
    nc = c.nc
    with ExitStack() as ex:
        sb = lambda n, s, d: ex.enter_context(nc.sbuf_tensor(pfx + n, s, d))
        a = [sb("a%d" % i, [128, 512], BF16) for i in range(2)]
        b = [sb("b%d" % i, [128, 512], BF16) for i in range(2)]
        aB = [Buf() for _ in range(2)]
        bB = [Buf() for _ in range(2)]
        q = c.new_dma_sem()
        qo = c.new_dma_sem()
        for p in range(ND):
            for ci in range(T // 512):
                j = ci % 2
                c.dma("sync", a[j][:], xnT_h.ap()[p * 128:(p + 1) * 128,
                                                  T - (ci + 1) * 512:T - ci * 512],
                      reads=(xnT_B,), writes=(aB[j],), sem=q)
                c.op("vector", lambda e, j=j: e.tensor_copy(b[j][:], a[j][:, ::-1]),
                     reads=(aB[j],), writes=(bB[j],))
                c.dma("sync", xnTr_h.ap()[p * 128:(p + 1) * 128,
                                          ci * 512:(ci + 1) * 512], b[j][:],
                      reads=(bB[j],), writes=(xnTr_B,), sem=qo)
        c.barrier()


def st_l3_mamba_dir(c, pfx, io, sfx, xin_h, xin_B, out_h, out_B):
    nc = c.nc
    DI = 2048
    NDI = 16
    CD = 2304
    NCD = 18
    NST = 128
    NH = 32
    HD = 64
    Q = 128
    TC = 512
    NCH = T // TC
    rev = (sfx == "b")
    WiT_h, WiT_B = c.win("l3a_WiT_" + sfx, [D, 4384], F32)
    cw_h, cw_B = c.win("l3a_cw_" + sfx, [128, 72], F32)
    cb_h, cb_B = c.win("l3a_cb_" + sfx, [128, 18], F32)
    dtb_h, dtb_B = c.win("l3a_dtb_" + sfx, [32, 1], F32)
    Al_h, Al_B = c.win("l3a_Alog_" + sfx, [32, 1], F32)
    Dd_h, Dd_B = c.win("l3a_D_" + sfx, [32, 1], F32)
    tri_h, tri_B = c.win("c_tri128", [128, 128], F32)
    WoTn_h, WoTn_B = io["l3a_WoTn_" + sfx]
    with ExitStack() as ex:
        sb = lambda n, s, d: ex.enter_context(nc.sbuf_tensor(pfx + n, s, d))
        pb = lambda n, s, d: ex.enter_context(nc.psum_tensor(pfx + n, s, d))
        wi = sb("wi", [128, ND * 4384], BF16)
        wiB = Buf()
        wo = sb("wo", [128, NDI * D], BF16)
        woB = Buf()
        cwc = sb("cwc", [128, 72], F32)
        cbc = sb("cbc", [128, 18], F32)
        cwcB, cbcB = Buf(), Buf()
        smallc = sb("smallc", [32, 8], F32)
        smallcB = Buf()
        tri = sb("tri", [128, 128], F32)
        triB = Buf()
        Dbc = sb("Dbc", [128, NH], F32)
        DbcB = Buf()
        xc = [sb("xc%d" % i, [128, ND * TC], BF16) for i in range(2)]
        xcB = [Buf() for _ in range(2)]
        xbc = [sb("xbc%d" % i, [128, TC + 3], F32) for i in range(2)]
        xbcB = [Buf() for _ in range(2)]
        halo = sb("halo", [128, NCD * 3], F32)
        haloB = [Buf() for _ in range(NCD)]
        xbs = sb("xbs", [128, NCD * TC], BF16)
        xbsB = [Buf() for _ in range(NCD)]
        sz = sb("sz", [128, 4 * DI], BF16)
        szB = [Buf() for _ in range(4)]
        dtr = sb("dtr", [32, TC], F32)
        dtrB = Buf()
        dts = sb("dts", [32, Q], F32)
        dtsB = Buf()
        aa = sb("aa", [64, Q], F32)
        aaB = Buf()
        tmp32 = sb("tmp32", [32, Q], F32)
        tmp32B = Buf()
        adt = sb("adt", [128, 64], F32)
        adtB = Buf()
        dcol = sb("dcol", [128, NH], F32)
        dcolB = Buf()
        ecol = sb("ecol", [128, NH], F32)
        ecolB = Buf()
        hsc = sb("hsc", [128, NH], F32)
        hscB = Buf()
        row0 = sb("row0", [1, 128], F32)
        row0B = Buf()
        row2 = [sb("row2%d" % i, [1, 128], F32) for i in range(2)]
        row2B_ = [Buf() for _ in range(2)]
        XT = sb("XT", [128, DI], BF16)
        XTB = Buf()
        BT = sb("BT", [128, 128], BF16)
        BTB = Buf()
        Bbar = [sb("Bbar%d" % i, [128, 128], BF16) for i in range(2)]
        BbarB_ = [Buf() for _ in range(2)]
        Gsb = sb("Gsb", [128, 128], F32)
        GsbB = Buf()
        dsb = [sb("dsb%d" % i, [128, 128], F32) for i in range(2)]
        dsbB_ = [Buf() for _ in range(2)]
        Wsb = [sb("Wsb%d" % i, [128, 128], BF16) for i in range(2)]
        WsbB_ = [Buf() for _ in range(2)]
        H = sb("H", [128, DI], F32)
        HB = [Buf() for _ in range(NH)]
        Hbf = sb("Hbf", [128, DI], BF16)
        HbfB = [Buf() for _ in range(NH)]
        ysb = sb("ysb", [128, DI], F32)
        ysbB = Buf()
        yi = [sb("yi%d" % i, [128, HD], F32) for i in range(2)]
        yiB_ = [Buf() for _ in range(2)]
        ynb = sb("ynb", [128, DI], BF16)
        ynbB = Buf()
        yT = sb("yT", [128, NDI * Q], BF16)
        yTB = [Buf() for _ in range(NDI)]
        st = sb("st", [128, 8], F32)
        stB = Buf()
        sq = sb("sq", [128, TC], F32)
        sqB = Buf()
        ev = [sb("ev%d" % i, [128, 512], F32) for i in range(2)]
        evB = [Buf() for _ in range(2)]
        ps = [pb("ps%d" % i, [128, 512], F32) for i in range(4)]
        psB = [Buf() for _ in range(6)]
        ps6 = pb("ps6", [128, 128], F32)
        ps6B = Buf()
        ps62 = pb("ps62", [128, 128], F32)
        ps62B = Buf()
        ps6L = [ps6, ps62]
        ps6LB = [ps6B, ps62B]
        ps7 = pb("ps7", [1, 128], F32)
        ps7B = Buf()
        ps72 = pb("ps72", [1, 128], F32)
        ps72B = Buf()
        ps7L = [ps7, ps72]
        ps7LB = [ps7B, ps72B]
        q = c.new_dma_sem()
        qw = c.new_dma_sem()
        qx = c.new_dma_sem()
        qo = c.new_dma_sem()

        for k in range(ND):
            c.dma("gpsimd", wi[:, k * 4384:(k + 1) * 4384],
                  WiT_h.ap()[k * 128:(k + 1) * 128, :],
                  reads=(WiT_B,), writes=(wiB,), sem=qw)
        for k in range(NDI):
            c.dma("gpsimd", wo[:, k * D:(k + 1) * D],
                  WoTn_h.ap()[k * 128:(k + 1) * 128, :],
                  reads=(WoTn_B,), writes=(woB,), sem=qw)
        c.dma("sync", cwc[:, :], cw_h.ap()[:, :], reads=(cw_B,), writes=(cwcB,), sem=q)
        c.dma("sync", cbc[:, :], cb_h.ap()[:, :], reads=(cb_B,), writes=(cbcB,), sem=q)
        c.dma("sync", tri[:, :], tri_h.ap()[:, :], reads=(tri_B,), writes=(triB,), sem=q)
        c.dma("sync", smallc[:, 0:1], dtb_h.ap()[:, :], reads=(dtb_B,),
              writes=(smallcB,), sem=q)
        c.dma("sync", smallc[:, 1:2], Al_h.ap()[:, :], reads=(Al_B,),
              writes=(smallcB,), sem=q)
        c.dma("sync", smallc[:, 2:3], Dd_h.ap()[:, :], reads=(Dd_B,),
              writes=(smallcB,), sem=q)
        c.op("scalar", lambda e: e.activation(out=smallc[:, 3:4], in_=smallc[:, 1:2],
                                              func=AF.Exp),
             reads=(smallcB,), writes=(smallcB,))
        c.op("vector", lambda e: e.tensor_scalar(smallc[:, 3:4], smallc[:, 3:4],
                                                 -1.0, None, ALU.mult),
             reads=(smallcB,), writes=(smallcB,))
        c.op("vector", lambda e: e.tensor_copy(smallc.bitcast(BF16)[0:32, 8:9],
                                               smallc[:, 2:3]),
             reads=(smallcB,), writes=(smallcB,))
        c.transpose(ps7B, ps7[0:1, 0:32].bitcast(BF16)[0:1, 0:32],
                    smallc.bitcast(BF16)[0:32, 8:9], reads=(smallcB,), nparts=32)
        c.op("vector", lambda e: e.tensor_copy(row0.bitcast(BF16)[0:1, 0:32],
                                               ps7[0:1, 0:32].bitcast(BF16)[0:1, 0:32]),
             reads=(ps7B,), writes=(row0B,))
        c.mm(ps6B, ps6[:, 0:32], [(c.consts["ones"][0:1, 0:128],
                                   row0.bitcast(BF16)[0:1, 0:32])],
             reads=(row0B,))
        c.op("vector", lambda e: e.tensor_copy(Dbc[:, :], ps6[:, 0:32]),
             reads=(ps6B,), writes=(DbcB,))
        c.op("vector", lambda e: e.memset(H[:, :], 0.0), writes=tuple(HB))
        c.op("vector", lambda e: e.memset(Hbf[:, :], 0.0), writes=tuple(HbfB))
        c.op("vector", lambda e: e.memset(halo[:, :], 0.0), writes=tuple(haloB))
        c.barrier()

        for ci in range(NCH):
            t0 = ci * TC
            jc = ci % 2
            for k in range(ND):
                c.dma("sync", xc[jc][:, k * TC:(k + 1) * TC],
                      xin_h.ap()[k * 128:(k + 1) * 128, t0:t0 + TC],
                      reads=(xin_B,), writes=(xcB[jc],), sem=qx)
            for tt in range(4):
                for nn in range(4):
                    p = nn % 2
                    c.mm(psB[p], ps[p][:, :],
                         [(xc[jc][:, k * TC + tt * 128:k * TC + (tt + 1) * 128],
                           wi[:, k * 4384 + nn * 512:k * 4384 + (nn + 1) * 512])
                          for k in range(ND)],
                         reads=(wiB, xcB[jc]))
                    c.op("scalar", lambda e, tt=tt, nn=nn, p=p: e.activation(
                        out=sz[:, tt * DI + nn * 512:tt * DI + (nn + 1) * 512],
                        in_=ps[p][:, :], func=AF.Silu),
                        reads=(psB[p],), writes=(szB[tt],))
            for m in range(NCD):
                jb = m % 2
                wk = xbc[jb]
                wkB = xbcB[jb]
                c.op("vector", lambda e, m=m, wk=wk: e.tensor_copy(
                    wk[:, 0:3], halo[:, m * 3:(m + 1) * 3]),
                    reads=(haloB[m],), writes=(wkB,))
                p = m % 2
                c.mm(psB[p], ps[p][:, :],
                     [(wi[:, k * 4384 + 2048 + m * 128:k * 4384 + 2048 + (m + 1) * 128],
                       xc[jc][:, k * TC:(k + 1) * TC]) for k in range(ND)],
                     reads=(wiB, xcB[jc]))
                c.op("vector", lambda e, p=p, wk=wk, wkB=wkB: e.tensor_copy(
                    wk[:, 3:3 + TC], ps[p][:, :]),
                    reads=(psB[p],), writes=(wkB,))
                c.op("vector", lambda e, m=m, wk=wk: e.tensor_copy(
                    halo[:, m * 3:(m + 1) * 3], wk[:, TC:TC + 3]),
                    reads=(wkB,), writes=(haloB[m],))
                c.op("vector", lambda e, m=m, wk=wk: e.tensor_scalar(
                    sq[:, 0:TC], wk[:, 0:TC], cwc[:, m:m + 1], None, ALU.mult),
                    reads=(wkB, cwcB), writes=(sqB,))
                for j in range(1, 4):
                    c.op("vector", lambda e, m=m, j=j, wk=wk: e.scalar_tensor_tensor(
                        sq[:, 0:TC], wk[:, j:j + TC],
                        cwc[:, j * 18 + m:j * 18 + m + 1], sq[:, 0:TC],
                        ALU.mult, ALU.add),
                        reads=(wkB, cwcB, sqB), writes=(sqB,))
                c.op("scalar", lambda e, m=m: e.activation(
                    out=xbs[:, m * TC:(m + 1) * TC], in_=sq[:, 0:TC],
                    func=AF.Silu, bias=cbc[:, m:m + 1]),
                    reads=(sqB, cbcB), writes=(xbsB[m],))
            for qq2 in range(4):
                c.waits("tensor", (wiB, xcB[jc]), (ps6B,))
                inst = None
                for k in range(ND):
                    inst = nc.tensor.matmul(
                        ps6[0:32, :],
                        wi[:, k * 4384 + 4352:k * 4384 + 4384],
                        xc[jc][:, k * TC + qq2 * 128:k * TC + (qq2 + 1) * 128],
                        start=(k == 0), stop=(k == ND - 1))
                    c.stats["instr"] += 1
                c.done("tensor", inst, (wiB, xcB[jc]), (ps6B,))
                c.op("vector", lambda e, qq2=qq2: e.tensor_copy(
                    dtr[:, qq2 * 128:(qq2 + 1) * 128], ps6[0:32, 0:128]),
                    reads=(ps6B,), writes=(dtrB,))
            for qq in range(4):
                cg = ci * 4 + qq
                tglob = t0 + qq * 128
                c.op("scalar", lambda e, qq=qq: e.activation(
                    out=dts[:, :], in_=dtr[:, qq * 128:(qq + 1) * 128],
                    func=AF.Identity, bias=smallc[0:32, 0:1]),
                    reads=(dtrB, smallcB), writes=(dtsB,))
                c.op("scalar", lambda e: e.activation(out=tmp32[:, :], in_=dts[:, :],
                                                      func=AF.Abs),
                     reads=(dtsB,), writes=(tmp32B,))
                c.op("scalar", lambda e: e.activation(out=tmp32[:, :], in_=tmp32[:, :],
                                                      func=AF.Exp, scale=-1.0),
                     reads=(tmp32B,), writes=(tmp32B,))
                c.op("scalar", lambda e: e.activation(out=tmp32[:, :], in_=tmp32[:, :],
                                                      func=AF.Ln, bias=1.0),
                     reads=(tmp32B,), writes=(tmp32B,))
                c.op("vector", lambda e: e.tensor_scalar(dts[:, :], dts[:, :], 0.0,
                                                         None, ALU.max),
                     reads=(dtsB,), writes=(dtsB,))
                c.op("vector", lambda e: e.tensor_add(dts[:, :], dts[:, :],
                                                      tmp32[:, :]),
                     reads=(tmp32B, dtsB), writes=(dtsB,))
                c.op("vector", lambda e: e.tensor_scalar(aa[0:32, :], dts[:, :],
                                                         smallc[0:32, 3:4], None,
                                                         ALU.mult),
                     reads=(dtsB, smallcB), writes=(aaB,))
                c.op("vector", lambda e: e.tensor_tensor_scan(
                    aa[0:32, :], aa[0:32, :], aa[0:32, :], 0.0, ALU.add, ALU.bypass),
                    reads=(aaB,), writes=(aaB,))
                c.op("vector", lambda e: e.tensor_copy(aa[32:64, :], dts[:, :]),
                     reads=(dtsB,), writes=(aaB,))
                c.transpose(ps6B, ps6[:, 0:64], aa[0:64, :], reads=(aaB,),
                            nparts=64, f32=True)
                c.op("vector", lambda e: e.tensor_copy(adt[:, :], ps6[:, 0:64]),
                     reads=(ps6B,), writes=(adtB,))
                c.transpose(ps7B, ps7[0:1, 0:32], aa[0:32, Q - 1:Q], reads=(aaB,),
                            nparts=32, f32=True)
                c.op("vector", lambda e: e.tensor_copy(row0[0:1, 0:32],
                                                       ps7[0:1, 0:32]),
                     reads=(ps7B,), writes=(row0B,))
                c.mm(ps6B, ps6[:, 64:96], [(c.consts["onesf"][0:1, 0:128],
                                            row0[0:1, 0:32])],
                     reads=(row0B,))
                c.op("scalar", lambda e: e.activation(out=hsc[:, :],
                                                      in_=ps6[:, 64:96], func=AF.Exp),
                     reads=(ps6B,), writes=(hscB,))
                c.op("vector", lambda e: e.tensor_sub(dcol[:, :], ps6[:, 64:96],
                                                      adt[:, 0:32]),
                     reads=(ps6B, adtB), writes=(dcolB,))
                c.op("scalar", lambda e: e.activation(out=dcol[:, :], in_=dcol[:, :],
                                                      func=AF.Exp),
                     reads=(dcolB,), writes=(dcolB,))
                c.op("vector", lambda e: e.tensor_mul(dcol[:, :], dcol[:, :],
                                                      adt[:, 32:64]),
                     reads=(adtB, dcolB), writes=(dcolB,))
                c.op("scalar", lambda e: e.activation(out=ecol[:, :],
                                                      in_=adt[:, 0:32], func=AF.Exp),
                     reads=(adtB,), writes=(ecolB,))
                for m in range(NDI):
                    pp = (m % 2)
                    c.transpose(psB[2 + pp], ps[2 + pp].bitcast(BF16)[:, 0:128],
                                xbs[:, m * TC + qq * 128:m * TC + (qq + 1) * 128],
                                reads=(xbsB[m],))
                    if rev:
                        c.op("vector", lambda e, m=m, pp=pp: e.tensor_copy(
                            XT[:, m * 128:(m + 1) * 128],
                            ps[2 + pp].bitcast(BF16)[:, 0:128]),
                            reads=(psB[2 + pp],), writes=(XTB,))
                    else:
                        c.op("vector", lambda e, m=m, pp=pp: e.tensor_copy(
                            XT[:, m * 128:(m + 1) * 128],
                            ps[2 + pp].bitcast(BF16)[:, 0:128]),
                            reads=(psB[2 + pp],), writes=(XTB,))
                Bfm = xbs[:, 16 * TC + qq * 128:16 * TC + (qq + 1) * 128]
                Cfm = xbs[:, 17 * TC + qq * 128:17 * TC + (qq + 1) * 128]
                c.mm(ps6B, ps6[:, 0:128], [(Bfm, Cfm)], reads=(xbsB[16], xbsB[17]))
                c.op("vector", lambda e: e.tensor_copy(Gsb[:, :], ps6[:, 0:128]),
                     reads=(ps6B,), writes=(GsbB,))
                c.transpose(psB[2], ps[2].bitcast(BF16)[:, 0:128], Bfm,
                            reads=(xbsB[16],))
                c.op("vector", lambda e: e.tensor_copy(BT[:, :],
                                                       ps[2].bitcast(BF16)[:, 0:128]),
                     reads=(psB[2],), writes=(BTB,))
                for h in range(NH):
                    jh = h % 2
                    dsb_, dsbB = dsb[jh], dsbB_[jh]
                    Wsb_, WsbB = Wsb[jh], WsbB_[jh]
                    Bbar_, BbarB = Bbar[jh], BbarB_[jh]
                    yi_, yiB = yi[jh], yiB_[jh]
                    row2_, row2B = row2[jh], row2B_[jh]
                    ps6_, ps6_B = ps6L[jh], ps6LB[jh]
                    ps7_, ps7_B = ps7L[jh], ps7LB[jh]
                    psA, psAB = ps[jh], psB[jh]
                    psC, psCB = ps[2 + jh], psB[2 + jh]
                    c.transpose(ps7_B, ps7_[0:1, :], adt[:, h:h + 1], reads=(adtB,),
                                nparts=128, f32=True)
                    c.op("vector", lambda e, ps7_=ps7_, row2_=row2_: e.tensor_copy(
                        row2_[0:1, :], ps7_[0:1, :]),
                        reads=(ps7_B,), writes=(row2B,))
                    c.mm(ps6_B, ps6_[:, 0:128], [(c.consts["onesf"][0:1, 0:128],
                                                  row2_[0:1, :])],
                         reads=(row2B,))
                    c.op("vector", lambda e, h=h, ps6_=ps6_, dsb_=dsb_:
                         e.scalar_tensor_tensor(
                             dsb_[:, :], ps6_[:, 0:128], adt[:, h:h + 1], tri[:, :],
                             ALU.subtract, ALU.add),
                         reads=(ps6_B, adtB, triB), writes=(dsbB,))
                    c.op("scalar", lambda e, dsb_=dsb_: e.activation(
                        out=dsb_[:, :], in_=dsb_[:, :], func=AF.Exp),
                        reads=(dsbB,), writes=(dsbB,))
                    c.op("vector", lambda e, h=h, dsb_=dsb_, Wsb_=Wsb_:
                         e.scalar_tensor_tensor(
                             Wsb_[:, :], dsb_[:, :], adt[:, 32 + h:33 + h], Gsb[:, :],
                             ALU.mult, ALU.mult),
                         reads=(dsbB, adtB, GsbB), writes=(WsbB,))
                    c.mm(psAB, psA[:, 0:HD],
                         [(Wsb_[:, :], XT[:, h * HD:(h + 1) * HD])],
                         reads=(WsbB, XTB))
                    c.op("vector", lambda e, yi_=yi_, psA=psA: e.tensor_copy(
                        yi_[:, :], psA[:, 0:HD]),
                        reads=(psAB,), writes=(yiB,))
                    c.mm(psCB, psC[:, 0:HD],
                         [(Cfm, Hbf[:, h * HD:(h + 1) * HD])],
                         reads=(xbsB[17], HbfB[h]))
                    c.op("vector", lambda e, h=h, psC=psC, yi_=yi_:
                         e.scalar_tensor_tensor(
                             ysb[:, h * HD:(h + 1) * HD], psC[:, 0:HD],
                             ecol[:, h:h + 1], yi_[:, :], ALU.mult, ALU.add),
                         reads=(psCB, ecolB, yiB), writes=(ysbB,))
                    c.op("vector", lambda e, h=h: e.scalar_tensor_tensor(
                        ysb[:, h * HD:(h + 1) * HD], XT[:, h * HD:(h + 1) * HD],
                        Dbc[:, h:h + 1], ysb[:, h * HD:(h + 1) * HD],
                        ALU.mult, ALU.add),
                        reads=(XTB, DbcB, ysbB), writes=(ysbB,))
                    c.op("vector", lambda e, h=h, Bbar_=Bbar_: e.tensor_scalar(
                        Bbar_[:, :], BT[:, :], dcol[:, h:h + 1], None, ALU.mult),
                        reads=(BTB, dcolB), writes=(BbarB,))
                    c.mm(psCB, psC[:, 0:HD],
                         [(Bbar_[:, :], XT[:, h * HD:(h + 1) * HD])],
                         reads=(BbarB, XTB))
                    c.op("vector", lambda e, h=h, psC=psC: e.scalar_tensor_tensor(
                        H[:, h * HD:(h + 1) * HD], H[:, h * HD:(h + 1) * HD],
                        hsc[:, h:h + 1], psC[:, 0:HD], ALU.mult, ALU.add),
                        reads=(hscB, psCB, HB[h]), writes=(HB[h],))
                    c.op("vector", lambda e, h=h: e.tensor_copy(
                        Hbf[:, h * HD:(h + 1) * HD], H[:, h * HD:(h + 1) * HD]),
                        reads=(HB[h],), writes=(HbfB[h],))
                c.op("vector", lambda e, qq=qq: e.tensor_mul(
                    ysb[:, :], ysb[:, :], sz[:, qq * DI:(qq + 1) * DI]),
                    reads=(szB[qq], ysbB), writes=(ysbB,))
                for p4 in range(4):
                    c.op("scalar", lambda e, p4=p4: e.activation(
                        out=sq[:, :], in_=ysb[:, p4 * 512:(p4 + 1) * 512],
                        func=AF.Square),
                        reads=(ysbB,), writes=(sqB,))
                    c.op("vector", lambda e, p4=p4: e.reduce_sum(
                        st[:, 3 + p4:4 + p4], sq[:, :], axis=AX.X),
                        reads=(sqB,), writes=(stB,))
                c.op("vector", lambda e: e.reduce_sum(st[:, 0:1], st[:, 3:7],
                                                      axis=AX.X),
                     reads=(stB,), writes=(stB,))
                epap, epB = c.cap(LN_EPS)
                c.op("scalar", lambda e: e.activation(out=st[:, 1:2], in_=st[:, 0:1],
                                                      func=AF.Sqrt, scale=1.0 / DI,
                                                      bias=epap),
                     reads=(stB, epB), writes=(stB,))
                c.op("vector", lambda e: e.reciprocal(st[:, 2:3], st[:, 1:2]),
                     reads=(stB,), writes=(stB,))
                c.op("vector", lambda e: e.tensor_scalar(ynb[:, :], ysb[:, :],
                                                         st[:, 2:3], None, ALU.mult),
                     reads=(ysbB, stB), writes=(ynbB,))
                for m in range(NDI):
                    pp = m % 2
                    c.transpose(psB[2 + pp], ps[2 + pp].bitcast(BF16)[:, 0:128],
                                ynb[:, m * 128:(m + 1) * 128], reads=(ynbB,))
                    if rev:
                        c.op("vector", lambda e, m=m, pp=pp: e.tensor_copy(
                            yT[:, m * Q:(m + 1) * Q],
                            ps[2 + pp].bitcast(BF16)[:, 0:128][:, ::-1]),
                            reads=(psB[2 + pp],), writes=(yTB[m],))
                    else:
                        c.op("vector", lambda e, m=m, pp=pp: e.tensor_copy(
                            yT[:, m * Q:(m + 1) * Q],
                            ps[2 + pp].bitcast(BF16)[:, 0:128]),
                            reads=(psB[2 + pp],), writes=(yTB[m],))
                tout = (T - tglob - 128) if rev else tglob
                for nn in range(2):
                    p = nn % 2
                    c.mm(psB[p], ps[p][:, :],
                         [(yT[:, k * Q:(k + 1) * Q],
                           wo[:, k * D + nn * 512:k * D + (nn + 1) * 512])
                          for k in range(NDI)],
                         reads=tuple([woB] + yTB))
                    c.op("vector", lambda e, p=p: e.tensor_copy(ev[p][:],
                                                                ps[p][:, :]),
                         reads=(psB[p],), writes=(evB[p],))
                    c.dma("sync", out_h.ap()[tout:tout + 128,
                                             nn * 512:(nn + 1) * 512], ev[p][:],
                          reads=(evB[p],), writes=(out_B,), sem=qo)
        c.barrier()



def build_program():
    nc = bass.Bass("TRN2", target_bir_lowering=False)
    c = Ctx(nc)
    x_in, x_inB = c.win("x_in", [T, D], F32)
    x_out, x_outB = c.win("x_out", [T, D], F32, kind="ExternalOutput")
    x_h, xr0 = c.win("x_res", [T, D], F32, kind="Internal")
    x_B = [xr0] + [Buf() for _ in range(NT - 1)]
    xnT, xnTB = c.win("s_xnT", [D, T], BF16, kind="Internal")
    xnTr, xnTrB = c.win("s_xnTr", [D, T], BF16, kind="Internal")
    xn_tm = c.win("s_xn_tm", [T, D], F32, kind="Internal")
    xnrT = c.win("s_xnrT", [D, T], BF16, kind="Internal")
    acc_h, _accB0 = c.win("s_acc", [T, D], F32, kind="Internal")
    accB = [Buf() for _ in range(NT * 2)]
    of_h, of_B = c.win("s_outf", [T, D], F32, kind="Internal")
    ob_h, ob_B = c.win("s_outb", [T, D], F32, kind="Internal")
    io = {
        "l1_wgT": c.win("s_l1_wgT", [D, 3072], BF16, kind="Internal"),
        "l1_wuT": c.win("s_l1_wuT", [D, 3072], BF16, kind="Internal"),
        "l1_wdT": c.win("s_l1_wdT", [3072, D], BF16, kind="Internal"),
        "l0_m2c": c.win("s_l0_m2c", [16, 256 * 64], BF16, kind="Internal"),
        "l2a_WkL": c.win("s_l2a_WkL", [D, D], BF16, kind="Internal"),
        "l2a_WvL": c.win("s_l2a_WvL", [D, D], BF16, kind="Internal"),
        "l2a_bkl": c.win("s_l2a_bkl", [1, D], F32, kind="Internal"),
        "l2a_bvl": c.win("s_l2a_bvl", [1, D], F32, kind="Internal"),
        "l2a_qT": c.win("s_qT", [D, T], BF16, kind="Internal"),
        "l2a_kT": c.win("s_kT", [D, T], BF16, kind="Internal"),
        "l2a_vT": c.win("s_vT", [D, T], BF16, kind="Internal"),
        "l1a_qT": c.win("s_qT", [D, T], BF16, kind="Internal"),
        "l1a_kT": c.win("s_kT", [D, T], BF16, kind="Internal"),
        "l1a_vT": c.win("s_vT", [D, T], BF16, kind="Internal"),
        "l3a_WoTn_f": c.win("s_l3a_WoTn_f", [2048, D], BF16, kind="Internal"),
        "l3a_WoTn_b": c.win("s_l3a_WoTn_b", [2048, D], BF16, kind="Internal"),
        "l0a_fq_re": c.win("s_fq_re", [D, D], BF16, kind="Internal"),
        "l0a_fq_im": c.win("s_fq_im", [D, D], BF16, kind="Internal"),
        "l0a_fk_re": c.win("s_fk_re", [D, D], BF16, kind="Internal"),
        "l0a_fk_im": c.win("s_fk_im", [D, D], BF16, kind="Internal"),
        "l0a_fv_re": c.win("s_fv_re", [D, D], BF16, kind="Internal"),
        "l0a_fv_im": c.win("s_fv_im", [D, D], BF16, kind="Internal"),
    }
    make_consts(c)
    q0 = c.new_dma_sem()
    c.dma("sync", x_h.ap()[:, :], x_in.ap()[:, :], reads=(x_inB,),
          writes=tuple(x_B), sem=q0)
    c.barrier()
    st_l1_dequant(c, "pdq_", io)
    st_l0_prep(c, "pl0_", io)
    st_l0a_prep(c, "pl0a_", io)
    st_l2a_prep(c, "pl2a_", io)
    st_l3a_prep(c, "pl3a_", io)

    def norm(pfx, **kw):
        st_norm(c, pfx, x_h, x_B, xnT, xnTB, **kw)

    norm("n00_")
    st_l0_ffn(c, "l0f1_", io, x_h, x_B, xnT, xnTB, alpha=0.5)
    norm("n01_")
    st_l0_attn(c, "l0a_", io, x_h, x_B, xnT, xnTB)
    norm("n02_")
    st_l0_ffn(c, "l0f2_", io, x_h, x_B, xnT, xnTB, alpha=0.5)
    norm("n10_")
    st_l1_ffn(c, "l1f1_", io, x_h, x_B, xnT, xnTB, alpha=0.5)
    norm("n11_")
    st_l1_attn(c, "l1a_", io, x_h, x_B, xnT, xnTB)
    norm("n12_")
    st_l1_ffn(c, "l1f2_", io, x_h, x_B, xnT, xnTB, alpha=0.5)
    norm("n20_", xn_tm=xn_tm)
    sub_expert_ffn(c, "l2f1_", "l2", x_h, x_B, xnT, xnTB, acc_h, accB,
                   xn_tm=xn_tm)
    norm("n21_")
    st_l2_attn(c, "l2a_", io, x_h, x_B, xnT, xnTB)
    norm("n22_", xn_tm=xn_tm)
    sub_expert_ffn(c, "l2f2_", "l2", x_h, x_B, xnT, xnTB, acc_h, accB,
                   xn_tm=xn_tm)
    norm("n30_", xn_tm=xn_tm, xnrT=xnrT)
    sub_expert_ffn(c, "l3f1_", "l3", x_h, x_B, xnT, xnTB, acc_h, accB,
                   xn_tm=xn_tm, xnrT=xnrT)
    norm("n31_")
    st_l3a_reverse(c, "l3r_", xnT, xnTB, xnTr, xnTrB)
    st_l3_mamba_dir(c, "l3mf_", io, "f", xnT, xnTB, of_h, of_B)
    st_l3_mamba_dir(c, "l3mb_", io, "b", xnTr, xnTrB, ob_h, ob_B)
    st_residual_add(c, "l3ra_", x_h, x_B, of_h, [of_B] * (NT * 2), 0.5,
                    extra=(ob_h, ob_B), extra_alpha=0.5)
    norm("n32_", xn_tm=xn_tm, xnrT=xnrT)
    sub_expert_ffn(c, "l3f2_", "l3", x_h, x_B, xnT, xnTB, acc_h, accB,
                   xn_tm=xn_tm, xnrT=xnrT)
    qf = c.new_dma_sem()
    c.waits("sync", reads=tuple(x_B))
    c.dma("sync", x_out.ap()[:, :], x_h.ap()[:, :], reads=tuple(x_B),
          writes=(x_outB,), sem=qf)
    c.barrier()
    return nc, c


_PROG_CACHE = {}


def kernel(**inputs):
    if "prog" not in _PROG_CACHE:
        _PROG_CACHE["prog"] = build_program()
    nc, c = _PROG_CACHE["prog"]
    names = c.external_input_names()
    shared = {}
    for n in names:
        if n == "x_in":
            continue
        shared[n] = HOSTPREP[n](inputs)
    x_full = np.asarray(inputs["L0_attn_x"], dtype=np.float32)
    in_maps = []
    for core in range(8):
        m = dict(shared)
        m["x_in"] = np.ascontiguousarray(x_full[core])
        in_maps.append(m)
    res = run_bass_kernel_spmd(nc, in_maps, core_ids=list(range(8)))
    return np.stack([r["x_out"] for r in res.results], axis=0)
